# version 02
# baseline (speedup 1.0000x reference)
import numpy as np
import concourse.bass as bass
import concourse.mybir as mybir
from concourse.bass_utils import run_bass_kernel_spmd
from contextlib import ExitStack
from ml_dtypes import bfloat16

F32 = mybir.dt.float32
BF16 = mybir.dt.bfloat16
AF = mybir.ActivationFunctionType
ALU = mybir.AluOpType
AX = mybir.AxisListType

NCORES = 8
T = 2048
TF = 4096
D = 1024
FC = 8
EPS = 1e-5

XT = 2 + T + 2 + T
OWN0 = 2
OTH0 = 2 + T + 2

S_LD = "sLD"
S_WD = "sWD"
S_MM = "sMM"
S_DV = "sDV"
S_AC = "sAC"
S_ST = "sST"


class Prog:
    ENGS = ("sync", "gpsimd", "scalar", "vector", "tensor")

    def __init__(self, nc):
        self.nc = nc
        self.cnt = {}
        self.handles = {}
        self.steps = {e: [] for e in self.ENGS}
        self.sem_names = []

    def sem(self, name):
        if name not in self.cnt:
            self.cnt[name] = 0
            self.sem_names.append(name)
        return name

    def inc(self, name, by=1):
        self.sem(name)
        self.cnt[name] += by
        return self.cnt[name]

    def val(self, name):
        self.sem(name)
        return self.cnt[name]

    def on(self, eng, fn):
        self.steps[eng].append(fn)

    def emit(self, ctx):
        for name in self.sem_names:
            self.handles[name] = ctx.enter_context(self.nc.semaphore(name))
        self.handles["_VC"] = ctx.enter_context(self.nc.semaphore("chainV"))
        self.handles["_vcn"] = [0]
        self.handles["_AC2"] = ctx.enter_context(self.nc.semaphore("chainA"))
        self.handles["_acn"] = [0]
        block = ctx.enter_context(self.nc.Block())
        p = self

        @block.sync
        def _(e):
            for fn in p.steps["sync"]:
                fn(e, p.handles)

        @block.gpsimd
        def _(e):
            for fn in p.steps["gpsimd"]:
                fn(e, p.handles)

        @block.scalar
        def _(e):
            for fn in p.steps["scalar"]:
                fn(e, p.handles)

        @block.vector
        def _(e):
            for fn in p.steps["vector"]:
                fn(e, p.handles)

        @block.tensor
        def _(e):
            for fn in p.steps["tensor"]:
                fn(e, p.handles)


def vgate(e, H, inst):
    H["_vcn"][0] += 1
    inst.then_inc(H["_VC"], 1)
    e.wait_ge(H["_VC"], H["_vcn"][0])


def agate(e, H, inst):
    H["_acn"][0] += 1
    inst.then_inc(H["_AC2"], 1)
    e.wait_ge(H["_AC2"], H["_acn"][0])


def to_fm(a):
    t, c = a.shape
    return np.ascontiguousarray(a.T.reshape(c // 128, 128, t).transpose(1, 0, 2))


def from_fm(a):
    return np.ascontiguousarray(a.transpose(1, 0, 2).reshape(-1, a.shape[2]).T)


def kmaj(w):
    k, m = w.shape
    return np.ascontiguousarray(w.reshape(k // 128, 128, m).transpose(1, 0, 2))


def colv(v):
    return np.ascontiguousarray(np.asarray(v, np.float32).reshape(FC, 128).T)



class Arena:
    def __init__(self, tensor, ncols):
        self.t = tensor
        self.n = ncols
        self.free = [(0, ncols)]

    def alloc(self, ncols, align=16):
        for i, (s0, ln) in enumerate(self.free):
            s = (s0 + align - 1) // align * align
            pad = s - s0
            if ln - pad >= ncols:
                rest = ln - pad - ncols
                self.free.pop(i)
                if pad:
                    self.free.insert(i, (s0, pad))
                    i += 1
                if rest:
                    self.free.insert(i, (s + ncols, rest))
                return Tile(self, s, ncols)
        raise MemoryError(f"arena out of space: need {ncols}, free={self.free}")

    def free_(self, tile):
        s, n = tile.off, tile.ncols
        self.free.append((s, n))
        self.free.sort()
        merged = []
        for a, b in self.free:
            if merged and merged[-1][0] + merged[-1][1] == a:
                merged[-1] = (merged[-1][0], merged[-1][1] + b)
            else:
                merged.append((a, b))
        self.free = [tuple(x) for x in merged]


class Tile:
    def __init__(self, arena, off, ncols):
        self.arena = arena
        self.off = off
        self.ncols = ncols

    def __getitem__(self, idx):
        pidx, cidx = idx
        c0 = cidx.start if cidx.start is not None else 0
        c1 = cidx.stop if cidx.stop is not None else self.ncols
        assert 0 <= c0 <= c1 <= self.ncols, (c0, c1, self.ncols)
        return self.arena.t[pidx, self.off + c0:self.off + c1]

    def ap(self):
        return self.arena.t[:, self.off:self.off + self.ncols]

    def free(self):
        self.arena.free_(self)




def emit_doubling_scan(e, H, src, dst, ewp, L):
    s_ = 1
    j = 0
    while s_ < L:
        ci = e.tensor_copy(dst[:, 0:s_], src[:, 0:s_])
        if s_ <= 256:
            vgate(e, H, ci)
        e.scalar_tensor_tensor(out=dst[:, s_:L], in0=src[:, 0:L - s_],
                               scalar=ewp[:, j:j + 1], op0=ALU.mult,
                               in1=src[:, s_:L], op1=ALU.add)
        src, dst = dst, src
        s_ *= 2
        j += 1
    return src

def build_phase_A():
    nc = bass.Bass("TRN2", target_bir_lowering=False, detect_race_conditions=False)
    x_in = nc.dram_tensor("x", [128, FC, TF], F32, kind="ExternalInput").ap()
    wk_in = nc.dram_tensor("wk", [128, FC, D], F32, kind="ExternalInput").ap()
    wv_in = nc.dram_tensor("wv", [128, FC, D], F32, kind="ExternalInput").ap()
    wr_in = nc.dram_tensor("wr", [128, FC, D], F32, kind="ExternalInput").ap()
    wo_in = nc.dram_tensor("wo", [128, FC, D], F32, kind="ExternalInput").ap()
    cc_in = nc.dram_tensor("cc", [128, 6 * FC + 8], F32, kind="ExternalInput").ap()
    or_in = nc.dram_tensor("onesrow", [1, 128], F32, kind="ExternalInput").ap()
    oc_in = nc.dram_tensor("onescol", [128, 1], F32, kind="ExternalInput").ap()
    xn_out = nc.dram_tensor("xn", [128, FC, XT], BF16, kind="ExternalOutput").ap()
    st_out = nc.dram_tensor("st", [1, 2 * TF], F32, kind="ExternalOutput").ap()
    s0_out = nc.dram_tensor("s0", [128, FC, T], F32, kind="ExternalOutput").ap()
    srq_d = nc.dram_tensor("srq", [128, FC, T], F32, kind="ExternalOutput").ap()
    eko_d = nc.dram_tensor("eko", [128, FC, T], F32, kind="ExternalOutput").ap()
    ekp_d = nc.dram_tensor("ekp", [128, FC, T], F32, kind="ExternalOutput").ap()
    rwkv_d = nc.dram_tensor("rwkvd", [128, FC, T], BF16, kind="ExternalOutput").ap()

    C_TMK, C_TMV, C_TMR, C_EW, C_EU = 0, FC, 2 * FC, 3 * FC, 4 * FC
    C_PM, C_PM1 = 6 * FC, 6 * FC + 1
    C_EPS, C_INVD, C_NEG1 = 6 * FC + 2, 6 * FC + 3, 6 * FC + 4
    Q = 1024

    p = Prog(nc)
    ctx = ExitStack()
    ccs = ctx.enter_context(nc.sbuf_tensor("ccs", [128, 6 * FC + 8], F32))
    orow = ctx.enter_context(nc.sbuf_tensor("orow", [1, 128], F32))
    ocol = ctx.enter_context(nc.sbuf_tensor("ocol", [128, 1], F32))
    init2 = ctx.enter_context(nc.sbuf_tensor("init2", [128, 2 * FC], F32))
    zt = ctx.enter_context(nc.sbuf_tensor("zt", [128, 512], F32))
    arF_t = ctx.enter_context(nc.sbuf_tensor("arF", [128, 21000], F32))
    arB_t = ctx.enter_context(nc.sbuf_tensor("arB", [128, 58000], BF16))
    psA = ctx.enter_context(nc.psum_tensor("psA", [128, 8 * 512], F32))
    arF = Arena(arF_t, 21000)
    arB = Arena(arB_t, 58000)

    xn = arB.alloc(FC * XT)
    wb = arB.alloc(FC * D)

    def xns(fc, c0, n):
        return xn[:, fc * XT + c0: fc * XT + c0 + n]

    def wsl(kc, m0, n=128):
        return wb[:, kc * D + m0: kc * D + m0 + n]

    v_cc = p.inc(S_LD, 16 * 3)

    def ld0(e, H):
        e.dma_start(out=ccs[:], in_=cc_in).then_inc(H[S_LD], 16)
        e.dma_start(out=orow[:], in_=or_in).then_inc(H[S_LD], 16)
        e.dma_start(out=ocol[:], in_=oc_in).then_inc(H[S_LD], 16)
    p.on("sync", ld0)

    v_z = p.inc(S_DV)

    def vz(e, H):
        e.memset(xn.ap(), 0.0).then_inc(H[S_DV], 1)
    p.on("vector", vz)

    v_euew = p.inc(S_AC)

    def a_pre(e, H, v_cc=v_cc):
        e.wait_ge(H[S_LD], v_cc)
        agate(e, H, e.activation(out=ccs[:, C_EW:C_EW + FC], in_=ccs[:, C_EW:C_EW + FC],
                                 func=AF.Exp))
        agate(e, H, e.activation(out=ccs[:, C_EW:C_EW + FC], in_=ccs[:, C_EW:C_EW + FC],
                                 func=AF.Exp, scale=ccs[:, C_NEG1:C_NEG1 + 1]))
        e.activation(out=ccs[:, C_EU:C_EU + FC], in_=ccs[:, C_EU:C_EU + FC],
                     func=AF.Exp).then_inc(H[S_AC], 1)
    p.on("scalar", a_pre)

    xs = arF.alloc(FC * Q)
    sq = arF.alloc(2 * Q)
    rows = arF.alloc(2 * Q)
    rows2 = arF.alloc(2 * Q)

    def ln_quarter(qi, wait_prev_dv):
        half, sub = qi // 2, qi % 2
        out0 = (OWN0 if half == 0 else OTH0) + sub * Q
        NG = Q // 512
        v_x = p.inc(S_LD, 16)

        def ldx(e, H, qi=qi, wait_prev_dv=wait_prev_dv):
            if wait_prev_dv is not None:
                e.wait_ge(H[S_DV], wait_prev_dv)
            e.dma_start(out=xs.ap().rearrange("q (c t) -> q c t", c=FC),
                        in_=x_in[:, :, qi * Q:(qi + 1) * Q]).then_inc(H[S_LD], 16)
        p.on("sync", ldx)

        v_r1 = p.inc(S_MM)

        def t1(e, H, v_x=v_x, wait_prev_dv=wait_prev_dv):
            e.wait_ge(H[S_LD], v_x)
            if wait_prev_dv is not None:
                e.wait_ge(H[S_DV], wait_prev_dv)
            for fc in range(FC):
                for g in range(NG):
                    mi = e.matmul(psA[0:1, g * 512:(g + 1) * 512], ocol[:, 0:1],
                                  xs[:, fc * Q + g * 512: fc * Q + (g + 1) * 512],
                                  start=(fc == 0), stop=(fc == FC - 1))
            mi.then_inc(H[S_MM], 1)
        p.on("tensor", t1)

        v_e1 = p.inc(S_DV)

        def v1(e, H, v_r1=v_r1):
            e.wait_ge(H[S_MM], v_r1)
            e.tensor_copy(rows[0:1, 0:Q], psA[0:1, 0:Q]).then_inc(H[S_DV], 1)
        p.on("vector", v1)

        v_sq = [p.inc(S_AC) for _ in range(FC)]
        v_r2 = [p.inc(S_MM) for _ in range(FC)]

        def a2(e, H, v_x=v_x, v_r2=tuple(v_r2)):
            e.wait_ge(H[S_LD], v_x)
            for fc in range(FC):
                if fc >= 2:
                    e.wait_ge(H[S_MM], v_r2[fc - 2])
                e.activation(out=sq[:, (fc % 2) * Q:(fc % 2) * Q + Q],
                             in_=xs[:, fc * Q:(fc + 1) * Q],
                             func=AF.Square).then_inc(H[S_AC], 1)
        p.on("scalar", a2)

        def t2(e, H, v_sq=tuple(v_sq), v_e1=v_e1):
            e.wait_ge(H[S_DV], v_e1)
            for fc in range(FC):
                e.wait_ge(H[S_AC], v_sq[fc])
                for g in range(NG):
                    mi = e.matmul(psA[0:1, g * 512:(g + 1) * 512], ocol[:, 0:1],
                                  sq[:, (fc % 2) * Q + g * 512:(fc % 2) * Q + (g + 1) * 512],
                                  start=(fc == 0), stop=(fc == FC - 1))
                mi.then_inc(H[S_MM], 1)
        p.on("tensor", t2)

        v_e2 = p.inc(S_DV)

        def v2(e, H, v_last=v_r2[-1]):
            e.wait_ge(H[S_MM], v_last)
            e.tensor_copy(rows[0:1, Q:2 * Q], psA[0:1, 0:Q]).then_inc(H[S_DV], 1)
        p.on("vector", v2)

        v_mu = p.inc(S_AC)
        v_musq = p.inc(S_AC)

        def a3(e, H, v_e2=v_e2):
            e.wait_ge(H[S_DV], v_e2)
            e.activation(out=rows2[0:1, 0:Q], in_=rows[0:1, 0:Q],
                         func=AF.Identity, scale=ccs[0:1, C_INVD:C_INVD + 1]).then_inc(H[S_AC], 1)
            e.activation(out=rows2[0:1, Q:2 * Q], in_=rows2[0:1, 0:Q],
                         func=AF.Square).then_inc(H[S_AC], 1)
        p.on("scalar", a3)

        v_var = p.inc(S_DV)

        def v3(e, H, v_musq=v_musq):
            e.wait_ge(H[S_AC], v_musq)
            e.scalar_tensor_tensor(out=rows2[0:1, Q:2 * Q], in0=rows[0:1, Q:2 * Q],
                                   scalar=ccs[0:1, C_INVD:C_INVD + 1], op0=ALU.mult,
                                   in1=rows2[0:1, Q:2 * Q],
                                   op1=ALU.subtract).then_inc(H[S_DV], 1)
        p.on("vector", v3)

        v_sqrt = p.inc(S_AC)

        def a4(e, H, v_var=v_var):
            e.wait_ge(H[S_DV], v_var)
            e.activation(out=rows2[0:1, Q:2 * Q], in_=rows2[0:1, Q:2 * Q],
                         func=AF.Sqrt, bias=ccs[0:1, C_EPS:C_EPS + 1]).then_inc(H[S_AC], 1)
        p.on("scalar", a4)

        v_rcp = p.inc(S_DV)

        def v4(e, H, v_sqrt=v_sqrt):
            e.wait_ge(H[S_AC], v_sqrt)
            e.reciprocal(rows2[0:1, Q:2 * Q], rows2[0:1, Q:2 * Q]).then_inc(H[S_DV], 1)
        p.on("vector", v4)

        v_str = p.inc(S_ST, 32)

        def st_rows(e, H, v_rcp=v_rcp, qi=qi):
            e.wait_ge(H[S_DV], v_rcp)
            e.dma_start(out=st_out[0:1, qi * Q:(qi + 1) * Q],
                        in_=rows2[0:1, 0:Q]).then_inc(H[S_ST], 16)
            e.dma_start(out=st_out[0:1, TF + qi * Q: TF + (qi + 1) * Q],
                        in_=rows2[0:1, Q:2 * Q]).then_inc(H[S_ST], 16)
        p.on("sync", st_rows)

        v_map = [p.inc(S_MM) for _ in range(NG)]
        v_ap = [p.inc(S_DV) for _ in range(NG)]

        def t3(e, H, v_rcp=v_rcp):
            e.wait_ge(H[S_DV], v_rcp)
            for g in range(NG):
                mi = e.matmul(psA[:, g * 512:g * 512 + 512], orow[0:1, :],
                              rows2[0:1, g * 512:(g + 1) * 512], start=True, stop=True)
                mi = e.matmul(psA[:, (4 + g) * 512:(4 + g) * 512 + 512], orow[0:1, :],
                              rows2[0:1, Q + g * 512: Q + (g + 1) * 512],
                              start=True, stop=True)
                mi.then_inc(H[S_MM], 1)
        p.on("tensor", t3)

        def v5(e, H, v_map=tuple(v_map), v_z=v_z, out0=out0):
            e.wait_ge(H[S_DV], v_z)
            for g in range(NG):
                e.wait_ge(H[S_MM], v_map[g])
                mu_map = psA[:, g * 512:g * 512 + 512]
                rs_map = psA[:, (4 + g) * 512:(4 + g) * 512 + 512]
                for fc in range(FC):
                    e.tensor_tensor(out=zt[:, 0:512],
                                    in0=xs[:, fc * Q + g * 512: fc * Q + (g + 1) * 512],
                                    in1=mu_map, op=ALU.subtract)
                    vi = e.tensor_tensor(out=xns(fc, out0 + g * 512, 512),
                                         in0=zt[:, 0:512], in1=rs_map, op=ALU.mult)
                vi.then_inc(H[S_DV], 1)
        p.on("vector", v5)
        return v_ap[-1]

    vq = None
    for qi in range(4):
        vq = ln_quarter(qi, vq)
    v_ln = vq

    v_prev = p.inc(S_DV)

    def vprev(e, H, v_ln=v_ln):
        e.wait_ge(H[S_DV], v_ln)
        for fc in range(FC):
            vgate(e, H, e.tensor_scalar_mul(xns(fc, OWN0 - 1, 1), xns(fc, OTH0 + T - 1, 1),
                                            ccs[:, C_PM:C_PM + 1]))
            if fc < FC - 1:
                vgate(e, H, e.tensor_scalar_mul(xns(fc, OTH0 - 1, 1),
                                                xns(fc, OWN0 + T - 1, 1),
                                                ccs[:, C_PM1:C_PM1 + 1]))
            else:
                vi = e.tensor_scalar_mul(xns(fc, OTH0 - 1, 1), xns(fc, OWN0 + T - 1, 1),
                                         ccs[:, C_PM1:C_PM1 + 1])
                vi.then_inc(H[S_DV], 1)
    p.on("vector", vprev)

    v_stxn = p.inc(S_ST, 16)

    def stxn(e, H, v_prev=v_prev):
        e.wait_ge(H[S_DV], v_prev)
        e.dma_start(out=xn_out,
                    in_=xn.ap().rearrange("q (c t) -> q c t", c=FC)).then_inc(H[S_ST], 16)
    p.on("sync", stxn)

    xs.free(); sq.free(); rows.free(); rows2.free()

    def load_weight(src, wait_mm=None):
        v = p.inc(S_WD, 16)

        def g(e, H, src=src, wait_mm=wait_mm):
            if wait_mm is not None:
                e.wait_ge(H[S_MM], wait_mm)
            e.dma_start(out=wb.ap().rearrange("q (c d) -> q c d", c=FC),
                        in_=src).then_inc(H[S_WD], 16)
        p.on("gpsimd", g)
        return v

    v_wr = load_weight(wr_in)
    srs = arF.alloc(FC * 512)
    mg = arB.alloc(FC * 512)

    v_mixg, v_rmm, v_rsg, v_rst = [], [], [], []
    for g in range(4):
        v_mixg.append(p.inc(S_DV))
        for dc in range(FC):
            v_rmm.append(p.inc(S_MM))
            v_rsg.append(p.inc(S_AC))
        v_rst.append(p.inc(S_ST, 16))

    def v_mixr(e, H, v_prev=v_prev, v_rmm=tuple(v_rmm)):
        for g in range(4):
            if g == 0:
                e.wait_ge(H[S_DV], v_prev)
            else:
                e.wait_ge(H[S_MM], v_rmm[g * FC - 1])
            for fc in range(FC):
                c0 = OWN0 + g * 512
                e.tensor_tensor(out=zt[:, 0:512], in0=xns(fc, c0, 512),
                                in1=xns(fc, c0 - 1, 512), op=ALU.subtract)
                vi = e.scalar_tensor_tensor(out=mg[:, fc * 512:(fc + 1) * 512],
                                            in0=zt[:, 0:512],
                                            scalar=ccs[:, C_TMR + fc:C_TMR + fc + 1],
                                            op0=ALU.mult, in1=xns(fc, c0 - 1, 512),
                                            op1=ALU.add)
            vi.then_inc(H[S_DV], 1)
    p.on("vector", v_mixr)

    def t_r(e, H, v_wr=v_wr, v_mixg=tuple(v_mixg), v_rsg=tuple(v_rsg)):
        e.wait_ge(H[S_WD], v_wr)
        for g in range(4):
            e.wait_ge(H[S_DV], v_mixg[g])
            for dc in range(FC):
                idx = g * FC + dc
                if idx >= 8:
                    e.wait_ge(H[S_AC], v_rsg[idx - 8])
                bank = idx % 8
                for fc in range(FC):
                    lt = wsl(fc, dc * 128)
                    e.ldweights(lt)
                    mi = e.matmul(psA[:, bank * 512:bank * 512 + 512], lt,
                                  mg[:, fc * 512:(fc + 1) * 512],
                                  start=(fc == 0), stop=(fc == FC - 1))
                mi.then_inc(H[S_MM], 1)
    p.on("tensor", t_r)

    def a_r(e, H, v_rmm=tuple(v_rmm), v_rst=tuple(v_rst)):
        for g in range(4):
            if g >= 1:
                e.wait_ge(H[S_ST], v_rst[g - 1])
            for dc in range(FC):
                idx = g * FC + dc
                e.wait_ge(H[S_MM], v_rmm[idx])
                bank = idx % 8
                e.activation(out=srs[:, dc * 512:(dc + 1) * 512],
                             in_=psA[:, bank * 512:bank * 512 + 512],
                             func=AF.Sigmoid).then_inc(H[S_AC], 1)
    p.on("scalar", a_r)

    def st_r(e, H, v_rsg=tuple(v_rsg)):
        for g in range(4):
            e.wait_ge(H[S_AC], v_rsg[g * FC + FC - 1])
            e.dma_start(out=srq_d[:, :, g * 512:(g + 1) * 512],
                        in_=srs.ap().rearrange("q (c t) -> q c t", c=FC)).then_inc(H[S_ST], 16)
    p.on("sync", st_r)

    srs.free(); mg.free()

    mix = arB.alloc(FC * T)

    def mixs(fc, g=None):
        if g is None:
            return mix[:, fc * T:(fc + 1) * T]
        return mix[:, fc * T + g * 512: fc * T + g * 512 + 512]

    def build_mix(ccol, block, wait_mm):
        start = OWN0 if block == 0 else OTH0
        v = p.inc(S_DV)

        def f(e, H, ccol=ccol, start=start, wait_mm=wait_mm):
            if wait_mm is not None:
                e.wait_ge(H[S_MM], wait_mm)
            for fc in range(FC):
                for g in range(4):
                    c0 = start + g * 512
                    e.tensor_tensor(out=zt[:, 0:512], in0=xns(fc, c0, 512),
                                    in1=xns(fc, c0 - 1, 512), op=ALU.subtract)
                    vi = e.scalar_tensor_tensor(out=mixs(fc, g), in0=zt[:, 0:512],
                                                scalar=ccs[:, ccol + fc:ccol + fc + 1],
                                                op0=ALU.mult, in1=xns(fc, c0 - 1, 512),
                                                op1=ALU.add)
            vi.then_inc(H[S_DV], 1)
        p.on("vector", f)
        return v

    eks = arF.alloc(T)
    ekvs = arF.alloc(T)
    sca = arF.alloc(T)
    scb = arF.alloc(T)
    nums = arF.alloc(T)
    dens = arF.alloc(T)
    srm2 = arF.alloc(T)
    ewp = arF.alloc(16)

    def k_round(spill_dram, wait_dv_mix, v_wk, wait_dv_pre=None):
        vmm, vex, vst = [], [], []
        for dc in range(FC):
            for g in range(4):
                vmm.append(p.inc(S_MM))
                vex.append(p.inc(S_AC))
            vst.append(p.inc(S_ST, 16))

        def t_k(e, H, wait_dv_mix=wait_dv_mix, v_wk=v_wk, vex=tuple(vex)):
            e.wait_ge(H[S_WD], v_wk)
            e.wait_ge(H[S_DV], wait_dv_mix)
            for dc in range(FC):
                for g in range(4):
                    idx = dc * 4 + g
                    if idx >= 8:
                        e.wait_ge(H[S_AC], vex[idx - 8])
                    bank = idx % 8
                    for fc in range(FC):
                        lt = wsl(fc, dc * 128)
                        e.ldweights(lt)
                        mi = e.matmul(psA[:, bank * 512:bank * 512 + 512], lt,
                                      mixs(fc, g), start=(fc == 0), stop=(fc == FC - 1))
                    mi.then_inc(H[S_MM], 1)
        p.on("tensor", t_k)

        def a_k(e, H, vmm=tuple(vmm), vst=tuple(vst), wait_dv_pre=wait_dv_pre):
            if wait_dv_pre is not None:
                e.wait_ge(H[S_DV], wait_dv_pre)
            for dc in range(FC):
                if dc >= 1:
                    e.wait_ge(H[S_ST], vst[dc - 1])
                for g in range(4):
                    idx = dc * 4 + g
                    e.wait_ge(H[S_MM], vmm[idx])
                    bank = idx % 8
                    e.activation(out=eks[:, g * 512:(g + 1) * 512],
                                 in_=psA[:, bank * 512:bank * 512 + 512],
                                 func=AF.Exp).then_inc(H[S_AC], 1)
        p.on("scalar", a_k)

        def st_k(e, H, vex=tuple(vex), spill_dram=spill_dram):
            for dc in range(FC):
                e.wait_ge(H[S_AC], vex[dc * 4 + 3])
                e.dma_start(out=spill_dram[:, dc, :], in_=eks[:, 0:T]).then_inc(H[S_ST], 16)
        p.on("sync", st_k)
        return vmm[-1], vst

    def v_round(ek_dram, v_wv, wait_dv_mix, k_spill_st, combine):
        vmm, vld, vev, vscan = [], [], [], []
        for dc in range(FC):
            vld.append(p.inc(S_LD, 16 * (2 if combine else 1)))
            for g in range(4):
                vmm.append(p.inc(S_MM))
                vev.append(p.inc(S_DV))
            vscan.append(p.inc(S_DV))

        def ld_e(e, H, vscan=tuple(vscan), k_spill_st=tuple(k_spill_st),
                 ek_dram=ek_dram, combine=combine):
            for dc in range(FC):
                if dc == 0:
                    e.wait_ge(H[S_ST], k_spill_st[-1])
                else:
                    e.wait_ge(H[S_DV], vscan[dc - 1])
                e.dma_start(out=eks[:, 0:T], in_=ek_dram[:, dc, :]).then_inc(H[S_LD], 16)
                if combine:
                    e.dma_start(out=srm2.ap(), in_=srq_d[:, dc, :]).then_inc(H[S_LD], 16)
        p.on("sync", ld_e)

        def t_v(e, H, v_wv=v_wv, wait_dv_mix=wait_dv_mix, vev=tuple(vev)):
            e.wait_ge(H[S_WD], v_wv)
            e.wait_ge(H[S_DV], wait_dv_mix)
            for dc in range(FC):
                for g in range(4):
                    idx = dc * 4 + g
                    if idx >= 8:
                        e.wait_ge(H[S_DV], vev[idx - 8])
                    bank = idx % 8
                    for fc in range(FC):
                        lt = wsl(fc, dc * 128)
                        e.ldweights(lt)
                        mi = e.matmul(psA[:, bank * 512:bank * 512 + 512], lt,
                                      mixs(fc, g), start=(fc == 0), stop=(fc == FC - 1))
                    mi.then_inc(H[S_MM], 1)
        p.on("tensor", t_v)

        def v_v(e, H, vmm=tuple(vmm), vld=tuple(vld), combine=combine):
            e.wait_ge(H[S_AC], 1)
            for dc in range(FC):
                e.wait_ge(H[S_LD], vld[dc])
                vgate(e, H, e.tensor_copy(ewp[:, 0:1], ccs[:, C_EW + dc:C_EW + dc + 1]))
                for j in range(1, 12):
                    vgate(e, H, e.tensor_tensor(out=ewp[:, j:j + 1], in0=ewp[:, j - 1:j],
                                                in1=ewp[:, j - 1:j], op=ALU.mult))
                for g in range(4):
                    idx = dc * 4 + g
                    e.wait_ge(H[S_MM], vmm[idx])
                    bank = idx % 8
                    vi = e.tensor_tensor(out=ekvs[:, g * 512:(g + 1) * 512],
                                         in0=psA[:, bank * 512:bank * 512 + 512],
                                         in1=eks[:, g * 512:(g + 1) * 512],
                                         op=ALU.mult)
                    vi.then_inc(H[S_DV], 1)
                if not combine:
                    ra = emit_doubling_scan(e, H, ekvs, sca, ewp, T)
                    rb = emit_doubling_scan(e, H, eks, scb, ewp, T)
                    vgate(e, H, e.tensor_scalar_mul(init2[:, 2 * dc:2 * dc + 1],
                                                    ra[:, T - 1:T], ccs[:, C_PM:C_PM + 1]))
                    vi = e.tensor_scalar_mul(init2[:, 2 * dc + 1:2 * dc + 2],
                                             rb[:, T - 1:T], ccs[:, C_PM:C_PM + 1])
                    vi.then_inc(H[S_DV], 1)
                else:
                    e.tensor_scalar_mul(nums.ap(), ekvs[:, 0:T],
                                        ccs[:, C_EU + dc:C_EU + dc + 1])
                    e.tensor_scalar_mul(dens.ap(), eks[:, 0:T],
                                        ccs[:, C_EU + dc:C_EU + dc + 1])
                    vgate(e, H, e.scalar_tensor_tensor(out=ekvs[:, 0:1],
                                           in0=init2[:, 2 * dc:2 * dc + 1],
                                           scalar=ccs[:, C_EW + dc:C_EW + dc + 1],
                                           op0=ALU.mult, in1=ekvs[:, 0:1], op1=ALU.add))
                    vgate(e, H, e.scalar_tensor_tensor(out=eks[:, 0:1],
                                           in0=init2[:, 2 * dc + 1:2 * dc + 2],
                                           scalar=ccs[:, C_EW + dc:C_EW + dc + 1],
                                           op0=ALU.mult, in1=eks[:, 0:1], op1=ALU.add))
                    ra = emit_doubling_scan(e, H, ekvs, sca, ewp, T)
                    rb = emit_doubling_scan(e, H, eks, scb, ewp, T)
                    vgate(e, H, e.tensor_tensor(out=nums[:, 0:1], in0=nums[:, 0:1],
                                    in1=init2[:, 2 * dc:2 * dc + 1], op=ALU.add))
                    e.tensor_tensor(out=nums[:, 1:T], in0=nums[:, 1:T],
                                    in1=ra[:, 0:T - 1], op=ALU.add)
                    vgate(e, H, e.tensor_tensor(out=dens[:, 0:1], in0=dens[:, 0:1],
                                    in1=init2[:, 2 * dc + 1:2 * dc + 2], op=ALU.add))
                    e.tensor_tensor(out=dens[:, 1:T], in0=dens[:, 1:T],
                                    in1=rb[:, 0:T - 1], op=ALU.add)
                    e.reciprocal(dens.ap(), dens.ap())
                    e.tensor_tensor(out=nums.ap(), in0=nums.ap(), in1=dens.ap(),
                                    op=ALU.mult)
                    vi = e.tensor_tensor(out=rwkv[:, dc * T:(dc + 1) * T],
                                         in0=nums.ap(), in1=srm2.ap(), op=ALU.mult)
                    vi.then_inc(H[S_DV], 1)
        p.on("vector", v_v)
        return vmm[-1], vscan[-1]

    v_mix_ko = build_mix(C_TMK, 1, wait_mm=v_rmm[-1])
    v_wk1 = load_weight(wk_in, wait_mm=v_rmm[-1])
    v_kmm_o, v_kst_o = k_round(eko_d, v_mix_ko, v_wk1)
    v_mix_vo = build_mix(C_TMV, 1, wait_mm=v_kmm_o)
    v_wv1 = load_weight(wv_in, wait_mm=v_kmm_o)
    v_vmm_o, v_scan_o = v_round(eko_d, v_wv1, v_mix_vo, v_kst_o, combine=False)

    v_mix_kw = build_mix(C_TMK, 0, wait_mm=v_vmm_o)
    v_wk2 = load_weight(wk_in, wait_mm=v_vmm_o)
    v_kmm_w, v_kst_w = k_round(ekp_d, v_mix_kw, v_wk2, wait_dv_pre=v_scan_o)
    v_mix_vw = build_mix(C_TMV, 0, wait_mm=v_kmm_w)
    xn.free()
    rwkv = arB.alloc(FC * T)
    v_wv2 = load_weight(wv_in, wait_mm=v_kmm_w)
    v_vmm_w, v_scan_w = v_round(ekp_d, v_wv2, v_mix_vw, v_kst_w, combine=True)

    v_dbg = p.inc(S_ST, 16)

    def st_dbg(e, H, v_scan_w=v_scan_w):
        e.wait_ge(H[S_DV], v_scan_w)
        e.dma_start(out=rwkv_d, in_=rwkv.ap().rearrange("q (c t) -> q c t", c=FC)).then_inc(H[S_ST], 16)
    p.on("sync", st_dbg)

    v_wo = load_weight(wo_in, wait_mm=v_vmm_w)
    s0s = arF.alloc(4 * 512)

    v_omm, v_oev, v_ost = [], [], []
    for cc in range(FC):
        for g in range(4):
            v_omm.append(p.inc(S_MM))
            v_oev.append(p.inc(S_AC))
        v_ost.append(p.inc(S_ST, 16))

    def t_o(e, H, v_wo=v_wo, v_scan_w=v_scan_w, v_oev=tuple(v_oev)):
        e.wait_ge(H[S_WD], v_wo)
        e.wait_ge(H[S_DV], v_scan_w)
        for cc in range(FC):
            for g in range(4):
                idx = cc * 4 + g
                if idx >= 8:
                    e.wait_ge(H[S_AC], v_oev[idx - 8])
                bank = idx % 8
                for dc in range(FC):
                    lt = wsl(dc, cc * 128)
                    e.ldweights(lt)
                    mi = e.matmul(psA[:, bank * 512:bank * 512 + 512], lt,
                                  rwkv[:, dc * T + g * 512: dc * T + (g + 1) * 512],
                                  start=(dc == 0), stop=(dc == FC - 1))
                mi.then_inc(H[S_MM], 1)
    p.on("tensor", t_o)

    def a_o(e, H, v_omm=tuple(v_omm), v_ost=tuple(v_ost)):
        for cc in range(FC):
            if cc >= 1:
                e.wait_ge(H[S_ST], v_ost[cc - 1])
            for g in range(4):
                idx = cc * 4 + g
                e.wait_ge(H[S_MM], v_omm[idx])
                bank = idx % 8
                e.activation(out=s0s[:, g * 512:(g + 1) * 512],
                             in_=psA[:, bank * 512:bank * 512 + 512],
                             func=AF.Identity).then_inc(H[S_AC], 1)
    p.on("scalar", a_o)

    v_final = p.val(S_ST)

    def st_o(e, H, v_oev=tuple(v_oev), v_final=v_final):
        for cc in range(FC):
            e.wait_ge(H[S_AC], v_oev[cc * 4 + 3])
            e.dma_start(out=s0_out[:, cc, :], in_=s0s.ap()).then_inc(H[S_ST], 16)
        e.wait_ge(H[S_ST], v_final)
    p.on("sync", st_o)

    p.emit(ctx)
    return nc, ctx


def host_prep_A(inputs, core):
    b, h = core // 2, core % 2
    x = np.asarray(inputs["L0_attn_x"][b], np.float32)
    own = x[h * T:(h + 1) * T]
    oth = x[(1 - h) * T:(2 - h) * T]
    xc = np.concatenate([own, oth], 0)
    m = {"x": to_fm(xc)}
    m["wk"] = kmaj(np.asarray(inputs["L0_attn_Wk"], np.float32).T)
    m["wv"] = kmaj(np.asarray(inputs["L0_attn_Wv"], np.float32).T)
    m["wr"] = kmaj(np.asarray(inputs["L0_attn_Wr"], np.float32).T)
    m["wo"] = kmaj(np.asarray(inputs["L0_attn_Wo"], np.float32).T)
    cc = np.zeros((128, 6 * FC + 8), np.float32)
    cc[:, 6 * FC + 2] = EPS
    cc[:, 6 * FC + 3] = 1.0 / D
    cc[:, 6 * FC + 4] = -1.0
    cc[:, 0:FC] = colv(inputs["L0_attn_time_mix_k"])
    cc[:, FC:2 * FC] = colv(inputs["L0_attn_time_mix_v"])
    cc[:, 2 * FC:3 * FC] = colv(inputs["L0_attn_time_mix_r"])
    cc[:, 3 * FC:4 * FC] = colv(inputs["L0_attn_time_decay"])
    cc[:, 4 * FC:5 * FC] = colv(inputs["L0_attn_time_first"])
    cc[:, 6 * FC] = float(h)
    cc[:, 6 * FC + 1] = float(1 - h)
    m["cc"] = cc
    m["onesrow"] = np.ones((1, 128), np.float32)
    m["onescol"] = np.ones((128, 1), np.float32)
    return m



class SProg:
    def __init__(self, nc):
        self.nc = nc
        self.steps = []

    def step(self, eng, ndma=0):
        def deco(fn):
            self.steps.append((eng, fn, ndma))
            return None
        return deco

    def emit(self, ctx):
        tc = ctx.enter_context(self.nc.semaphore("tickC"))
        td = ctx.enter_context(self.nc.semaphore("tickD"))
        vg = ctx.enter_context(self.nc.semaphore("gateV"))
        ag = ctx.enter_context(self.nc.semaphore("gateA"))
        H = {"_VC": vg, "_vcn": [0], "_AC2": ag, "_acn": [0], "tc": tc, "td": td}
        nC = nD = 0
        waits = []
        for eng, fn, ndma in self.steps:
            waits.append((nC, nD))
            if ndma:
                nD += 16 * ndma
            else:
                nC += 1
        block = ctx.enter_context(self.nc.Block())
        sp = self

        def run(e, name):
            for i, (eng, fn, ndma) in enumerate(sp.steps):
                if eng != name:
                    continue
                wc, wd = waits[i]
                if wc:
                    e.wait_ge(tc, wc)
                if wd:
                    e.wait_ge(td, wd)
                insts = fn(e, H)
                if ndma:
                    assert isinstance(insts, (list, tuple)) and len(insts) == ndma
                    for it in insts:
                        it.then_inc(td, 16)
                else:
                    insts.then_inc(tc, 1)

        @block.sync
        def _(e):
            run(e, "sync")

        @block.gpsimd
        def _(e):
            run(e, "gpsimd")

        @block.scalar
        def _(e):
            run(e, "scalar")

        @block.vector
        def _(e):
            run(e, "vector")

        @block.tensor
        def _(e):
            run(e, "tensor")




def ffn_mod_dims(MOD):
    W = MOD["W"]
    d = dict(MOD)
    d["NPAIR"] = MOD["H"] // 2
    d["IC"] = MOD["INNER"] // 128
    d["QB"] = 256
    d["SPAN"] = W + 256
    d["NKC"] = d["SPAN"] // 128
    d["HIDC"] = (MOD["HID"] + 127) // 128
    d["HID2"] = d["HIDC"] * 128
    d["QEXT"] = 256
    d["KH"] = W + 256
    d["TQ"] = 256 + T
    d["TKY"] = d["KH"] + T
    d["NVC"] = d["TKY"] // 128
    d["HB"] = d["KH"] + 4
    d["TK"] = d["HB"] + T
    d["NSPEC"] = d["KH"] // 256
    return d


def build_phase_ffn(MOD):
    M = ffn_mod_dims(MOD)
    W = M["W"]; NH = M["H"]; INNER = M["INNER"]
    NPAIR = M["NPAIR"]; IC = M["IC"]; QB = M["QB"]; SPAN = M["SPAN"]
    NKC = M["NKC"]; HIDC = M["HIDC"]; HID2 = M["HID2"]; QEXT = M["QEXT"]
    KH = M["KH"]; TQ = M["TQ"]; TKY = M["TKY"]; NVC = M["NVC"]
    HB = M["HB"]; TK = M["TK"]; NSPEC = M["NSPEC"]
    NQB2 = T // QB + 1

    nc = bass.Bass("TRN2", target_bir_lowering=False, detect_race_conditions=False)

    def dt_in(name, shape, dt=F32):
        return nc.dram_tensor(name, shape, dt, kind="ExternalInput").ap()
    yin = dt_in("yin", [128, FC, T])
    hal = dt_in("hal", [128, FC, HB])
    mus = dt_in("mus", [2, HB + T])
    wqkv = dt_in("wqkv", [NPAIR, 128, FC, 384])
    wo = dt_in("wo", [128, IC, D])
    w1 = dt_in("w1", [128, FC, 2 * HID2])
    w2 = dt_in("w2", [128, HIDC, D])
    b1c = dt_in("b1c", [128, 2 * HIDC])
    cc2 = dt_in("cc2", [128, 64])
    rqc = dt_in("rqc", [128, TQ]); rqs = dt_in("rqs", [128, TQ])
    rkc = dt_in("rkc", [128, TKY]); rks = dt_in("rks", [128, TKY])
    mstr = dt_in("mstr", [128, (NSPEC + 1) * NKC * QB], BF16)
    mstrA = dt_in("mstrA", [128, (NSPEC + 1) * 2 * SPAN], BF16)
    hsp = dt_in("hsplit", [128, 2])
    spr = dt_in("spread", [128, 512])
    sel = dt_in("sel", [128, 384], BF16)
    ident_in = dt_in("ident", [128, 128])
    yout = nc.dram_tensor("yout", [128, FC, T], F32, kind="ExternalOutput").ap()
    avd = nc.dram_tensor("avd", [128, IC, 2052], BF16).ap()

    C2_LG, C2_LB, C2_B2 = 0, 8, 16
    C2_EPS12, C2_NEG1, C2_QKS, C2_INVD, C2_TINY, C2_LNEPS = 24, 25, 26, 27, 28, 29

    p = SProg(nc)
    ctx = ExitStack()
    ccs = ctx.enter_context(nc.sbuf_tensor("ccs2", [128, 64], F32))
    hsplit = ctx.enter_context(nc.sbuf_tensor("hsplit2", [128, 2], F32))
    spread = ctx.enter_context(nc.sbuf_tensor("spread2", [128, 512], F32))
    selm = ctx.enter_context(nc.sbuf_tensor("selm2", [128, 384], BF16))
    ident = ctx.enter_context(nc.sbuf_tensor("ident2", [128, 128], F32))
    zt = ctx.enter_context(nc.sbuf_tensor("zt2", [128, 512], F32))
    b1s = ctx.enter_context(nc.sbuf_tensor("b1s2", [128, 2 * HIDC], F32))
    arF_t = ctx.enter_context(nc.sbuf_tensor("arF2", [128, 21600], F32))
    arB_t = ctx.enter_context(nc.sbuf_tensor("arB2", [128, 60000], BF16))
    psA = ctx.enter_context(nc.psum_tensor("psA2", [128, 8 * 512], F32))
    arF = Arena(arF_t, 21600)
    arB = Arena(arB_t, 60000)

    y = arF.alloc(FC * T)
    sqs = arF.alloc(512)
    y1h = arF.alloc(FC)
    onesp = arF.alloc(64)
    musr = arF.alloc(HB + T)

    zb = arB.alloc(FC * TK)
    qpair = arB.alloc(TQ); kpair = arB.alloc(TKY)
    rotu = arB.alloc(TKY); rot2 = arB.alloc(TKY)
    cms = arB.alloc(TKY); sms = arB.alloc(TKY)
    qB = arB.alloc(TQ)
    kB = rotu
    wqb = arB.alloc(FC * 384)
    vtA = arB.alloc(NVC * 65); vtB = arB.alloc(NVC * 65)
    pTt = arB.alloc(NKC * QB)
    av1 = arB.alloc(QB); lbt = arB.alloc(QB)
    avh = arB.alloc(2 * TQ)
    msB = arB.alloc(NKC * QB)
    mAs = arB.alloc(2 * SPAN)
    avst = arB.alloc(512)

    def zbs(fc, c0, n):
        return zb[:, fc * TK + c0: fc * TK + c0 + n]

    Q0 = HB - QEXT
    K0 = HB - KH

    @p.step("sync", ndma=6)
    def _(e, H):
        return [e.dma_start(out=y.ap().rearrange("q (c t) -> q c t", c=FC), in_=yin),
                e.dma_start(out=ccs[:], in_=cc2),
                e.dma_start(out=hsplit[:], in_=hsp),
                e.dma_start(out=spread[:], in_=spr),
                e.dma_start(out=selm[:], in_=sel),
                e.dma_start(out=ident[:], in_=ident_in)]

    @p.step("sync", ndma=2)
    def _(e, H):
        return [e.dma_start(out=musr[0:2, 0:HB + T], in_=mus),
                e.dma_start(out=b1s[:], in_=b1c)]

    @p.step("vector")
    def _(e, H):
        e.memset(onesp.ap(), 1.0)
        return e.memset(avh.ap(), 0.0)

    for c0 in range(0, T, 512):
        @p.step("tensor")
        def _(e, H, c0=c0):
            e.matmul(psA[:, 2 * 512:2 * 512 + 512], spread[0:2, 256:384],
                     musr[0:2, HB + c0:HB + c0 + 512], start=True, stop=True)
            return e.matmul(psA[:, 3 * 512:3 * 512 + 512], spread[0:2, 384:512],
                            musr[0:2, HB + c0:HB + c0 + 512], start=True, stop=True)

        @p.step("vector")
        def _(e, H, c0=c0):
            for fc in range(FC):
                sl = y[:, fc * T + c0: fc * T + c0 + 512]
                e.tensor_tensor(out=sl, in0=sl, in1=psA[:, 2 * 512:2 * 512 + 512],
                                op=ALU.subtract)
                mi = e.tensor_tensor(out=sl, in0=sl, in1=psA[:, 3 * 512:3 * 512 + 512],
                                     op=ALU.mult)
            return mi

    HBL = [(i, min(512, HB - i)) for i in range(0, HB, 512)]
    for h0, n in HBL:
        @p.step("tensor")
        def _(e, H, h0=h0, n=n):
            e.matmul(psA[:, 2 * 512:2 * 512 + n], spread[0:2, 256:384],
                     musr[0:2, h0:h0 + n], start=True, stop=True)
            return e.matmul(psA[:, 3 * 512:3 * 512 + n], spread[0:2, 384:512],
                            musr[0:2, h0:h0 + n], start=True, stop=True)

        for fc in range(FC):
            @p.step("gpsimd", ndma=1)
            def _(e, H, h0=h0, n=n, fc=fc):
                return [e.dma_start(out=zt[:, 0:n], in_=hal[:, fc, h0:h0 + n])]

            @p.step("vector")
            def _(e, H, n=n):
                e.tensor_tensor(out=zt[:, 0:n], in0=zt[:, 0:n],
                                in1=psA[:, 2 * 512:2 * 512 + n], op=ALU.subtract)
                return e.tensor_tensor(out=zt[:, 0:n], in0=zt[:, 0:n],
                                       in1=psA[:, 3 * 512:3 * 512 + n], op=ALU.mult)

            @p.step("scalar")
            def _(e, H, h0=h0, n=n, fc=fc):
                off = 1 if fc >= 4 else 0
                return e.activation(out=zbs(fc, h0 + off, n), in_=zt[:, 0:n],
                                    func=AF.Identity)

    @p.step("vector")
    def _(e, H):
        for fc in range(4, FC):
            vgate(e, H, e.memset(zbs(fc, 0, 1), 0.0))
        return e.memset(zt[:, 0:256], 0.0)

    @p.step("scalar")
    def _(e, H):
        for fc in range(FC):
            if fc < 4:
                mi = e.activation(out=zbs(fc, HB, T), in_=y[:, fc * T:(fc + 1) * T],
                                  func=AF.Identity)
            else:
                mi = e.activation(out=zbs(fc, HB + 1, T - 1),
                                  in_=y[:, fc * T:fc * T + T - 1], func=AF.Identity)
        return mi

    @p.step("vector")
    def _(e, H):
        for fc in range(FC):
            src = zbs(fc, HB - 1 if fc < 4 else HB, 1)
            vgate(e, H, e.tensor_copy(y1h[:, fc:fc + 1], src))
        return e.tensor_copy(zt[:, 0:256], zt[:, 0:256])

    musr.free()
    nrm = arF.alloc(max(TQ, TKY))
    mtv = arF.alloc(16)
    lrec = arF.alloc(QB)

    TQBL = [(i, min(512, TQ - i)) for i in range(0, TQ, 512)]
    TKBL = [(i, min(512, TKY - i)) for i in range(0, TKY, 512)]

    def proj_side(col0, z0, blocks, dstpair, dstB, rc_in, rs_in, sprcol, XL):
        @p.step("gpsimd", ndma=2)
        def _(e, H):
            return [e.dma_start(out=cms[:, 0:XL], in_=rc_in[:, 0:XL]),
                    e.dma_start(out=sms[:, 0:XL], in_=rs_in[:, 0:XL])]

        for c0, n in blocks:
            @p.step("tensor")
            def _(e, H, c0=c0, n=n):
                for fc in range(FC):
                    lt = wqb[:, fc * 384 + col0: fc * 384 + col0 + 128]
                    e.ldweights(lt)
                    mi = e.matmul(psA[:, 0:n], lt, zbs(fc, z0 + c0, n),
                                  start=(fc == 0), stop=(fc == FC - 1))
                return mi

            @p.step("scalar")
            def _(e, H, c0=c0, n=n):
                e.activation(out=dstpair[:, c0:c0 + n], in_=psA[:, 0:n], func=AF.Identity)
                return e.activation(out=sqs[:, 0:n], in_=psA[:, 0:n], func=AF.Square)

            @p.step("tensor")
            def _(e, H, n=n):
                return e.matmul(psA[0:2, 2 * 512:2 * 512 + n], hsplit[:, 0:2],
                                sqs[:, 0:n], start=True, stop=True)

            @p.step("vector")
            def _(e, H, c0=c0, n=n):
                return e.tensor_copy(nrm[0:2, c0:c0 + n], psA[0:2, 2 * 512:2 * 512 + n])

        @p.step("scalar")
        def _(e, H):
            return e.activation(out=nrm[0:2, 0:XL], in_=nrm[0:2, 0:XL], func=AF.Sqrt)

        @p.step("vector")
        def _(e, H):
            vgate(e, H, e.tensor_scalar_max(nrm[0:2, 0:XL], nrm[0:2, 0:XL],
                                            ccs[0:2, C2_EPS12:C2_EPS12 + 1]))
            vgate(e, H, e.reciprocal(nrm[0:2, 0:XL], nrm[0:2, 0:XL]))
            return e.tensor_copy(zt[:, 0:256], zt[:, 0:256])

        for c0, n in blocks:
            @p.step("tensor")
            def _(e, H, c0=c0, n=n):
                return e.matmul(psA[:, 3 * 512:3 * 512 + n], spread[0:2, sprcol:sprcol + 128],
                                nrm[0:2, c0:c0 + n], start=True, stop=True)

            @p.step("vector")
            def _(e, H, c0=c0, n=n):
                return e.tensor_tensor(out=dstpair[:, c0:c0 + n], in0=dstpair[:, c0:c0 + n],
                                       in1=psA[:, 3 * 512:3 * 512 + n], op=ALU.mult)

        @p.step("vector")
        def _(e, H):
            return e.tensor_tensor(out=rotu[:, 0:XL], in0=dstpair[:, 0:XL],
                                   in1=cms[:, 0:XL], op=ALU.mult)

        @p.step("gpsimd", ndma=4)
        def _(e, H):
            outs = []
            for d0 in (0, 32, 64, 96):
                srp = d0 + 32 if (d0 % 64) == 0 else d0 - 32
                outs.append(e.dma_start(out=rot2[d0:d0 + 32, 0:XL],
                                        in_=dstpair[srp:srp + 32, 0:XL]))
            return outs

        @p.step("vector")
        def _(e, H):
            e.tensor_tensor(out=rot2[:, 0:XL], in0=rot2[:, 0:XL], in1=sms[:, 0:XL],
                            op=ALU.mult)
            return e.tensor_tensor(out=dstpair[:, 0:XL], in0=rotu[:, 0:XL],
                                   in1=rot2[:, 0:XL], op=ALU.add)

        for c0, n in blocks:
            @p.step("tensor")
            def _(e, H, c0=c0, n=n):
                e.ldweights(selm[:, 128:256])
                return e.matmul(psA[:, 0:n], selm[:, 128:256], dstpair[:, c0:c0 + n],
                                start=True, stop=True)

            @p.step("scalar")
            def _(e, H, c0=c0, n=n):
                return e.activation(out=dstB[:, c0:c0 + n], in_=psA[:, 0:n],
                                    func=AF.Identity)

    def emit_head(h):
        qh = qpair if h == 0 else qB
        kh = kpair if h == 0 else kB
        vh = vtA if h == 0 else vtB
        for qb in range(NQB2):
            mset = min(qb, NSPEC)
            if qb <= NSPEC:
                @p.step("sync", ndma=2)
                def _(e, H, mset=mset):
                    return [e.dma_start(out=msB.ap(),
                                        in_=mstr[:, mset * NKC * QB:(mset + 1) * NKC * QB]),
                            e.dma_start(out=mAs.ap(),
                                        in_=mstrA[:, mset * 2 * SPAN:(mset + 1) * 2 * SPAN])]

            for sub in range(2):
                qq = qb * QB + sub * 128

                @p.step("tensor")
                def _(e, H, qq=qq, qb=qb):
                    kc0 = qb * QB
                    done = 0
                    e.ldweights(qh[0:64, qq:qq + 128])
                    while done < SPAN:
                        n = min(512, SPAN - done)
                        mi = e.matmul(psA[:, done:done + n], qh[0:64, qq:qq + 128],
                                      kh[0:64, kc0 + done:kc0 + done + n],
                                      start=True, stop=True)
                        done += n
                    return mi

                @p.step("vector")
                def _(e, H, sub=sub):
                    e.tensor_tensor(out=psA[:, 0:SPAN], in0=psA[:, 0:SPAN],
                                    in1=mAs[:, sub * SPAN:(sub + 1) * SPAN], op=ALU.add)
                    return e.tensor_reduce(out=mtv[:, 0:1], in_=psA[:, 0:SPAN],
                                           op=ALU.max, axis=AX.X)

                @p.step("tensor")
                def _(e, H):
                    return e.matmul(psA[64:65, 7 * 512:7 * 512 + 128], mtv[:, 0:1],
                                    ident[:, 0:128], start=True, stop=True)

                @p.step("scalar")
                def _(e, H, qq=qq):
                    return e.activation(out=qh[64:65, qq:qq + 128],
                                        in_=psA[64:65, 7 * 512:7 * 512 + 128],
                                        func=AF.Identity,
                                        scale=ccs[64:65, C2_NEG1:C2_NEG1 + 1])

            for kc in range(NKC):
                @p.step("tensor")
                def _(e, H, qb=qb, kc=kc):
                    kcc = qb * QB + kc * 128
                    e.ldweights(kh[0:65, kcc:kcc + 128])
                    return e.matmul(psA[:, 4 * 512:4 * 512 + QB], kh[0:65, kcc:kcc + 128],
                                    qh[0:65, qb * QB: qb * QB + QB], start=True, stop=True)

                @p.step("vector")
                def _(e, H, kc=kc):
                    return e.tensor_tensor(out=psA[:, 4 * 512:4 * 512 + QB],
                                           in0=psA[:, 4 * 512:4 * 512 + QB],
                                           in1=msB[:, kc * QB:(kc + 1) * QB], op=ALU.add)

                @p.step("scalar")
                def _(e, H, kc=kc):
                    return e.activation(out=pTt[:, kc * QB:(kc + 1) * QB],
                                        in_=psA[:, 4 * 512:4 * 512 + QB], func=AF.Exp,
                                        scale=ccs[:, C2_QKS:C2_QKS + 1])

            @p.step("tensor")
            def _(e, H, qb=qb, vh=vh):
                for kc in range(NKC):
                    vc = qb * 2 + kc
                    e.ldweights(vh[:, vc * 65: vc * 65 + 65])
                    mi = e.matmul(psA[0:65, 5 * 512:5 * 512 + QB],
                                  vh[:, vc * 65:vc * 65 + 65],
                                  pTt[:, kc * QB:(kc + 1) * QB],
                                  start=(kc == 0), stop=(kc == NKC - 1))
                return mi

            @p.step("vector")
            def _(e, H):
                vgate(e, H, e.tensor_scalar_add(lrec[64:65, 0:QB],
                                                psA[64:65, 5 * 512:5 * 512 + QB],
                                                ccs[64:65, C2_TINY:C2_TINY + 1]))
                vgate(e, H, e.reciprocal(lrec[64:65, 0:QB], lrec[64:65, 0:QB]))
                return e.tensor_copy(zt[:, 0:256], zt[:, 0:256])

            @p.step("tensor")
            def _(e, H):
                return e.matmul(psA[0:64, 6 * 512:6 * 512 + QB], onesp[64:65, 0:64],
                                lrec[64:65, 0:QB], start=True, stop=True)

            @p.step("scalar")
            def _(e, H):
                e.activation(out=av1[0:64, 0:QB], in_=psA[0:64, 5 * 512:5 * 512 + QB],
                             func=AF.Identity)
                return e.activation(out=lbt[0:64, 0:QB], in_=psA[0:64, 6 * 512:6 * 512 + QB],
                                    func=AF.Identity)

            @p.step("vector")
            def _(e, H, qb=qb, h=h):
                return e.tensor_tensor(
                    out=avh[0:64, h * TQ + qb * QB: h * TQ + qb * QB + QB],
                    in0=av1[0:64, 0:QB], in1=lbt[0:64, 0:QB], op=ALU.mult)

    PKBL = [(QEXT - 1, 512), (QEXT + 511, 512), (QEXT + 1023, 512),
            (QEXT + 1535, 256), (QEXT + 1791, 257)]

    def emit_pair(hp):
        @p.step("gpsimd", ndma=1)
        def _(e, H, hp=hp):
            return [e.dma_start(out=wqb.ap().rearrange("q (c d) -> q c d", c=FC),
                                in_=wqkv[hp])]

        proj_side(0, Q0, TQBL, qpair, qB, rqc, rqs, 0, TQ)
        proj_side(128, K0, TKBL, kpair, kB, rkc, rks, 128, TKY)

        @p.step("vector")
        def _(e, H):
            vgate(e, H, e.memset(kpair[64:65, 0:TKY], 1.0))
            vgate(e, H, e.memset(kB[64:65, 0:TKY], 1.0))
            e.memset(vtA.ap(), 1.0)
            return e.memset(vtB.ap(), 1.0)

        for vc in range(NVC):
            @p.step("tensor")
            def _(e, H, vc=vc):
                for fc in range(FC):
                    lt = zbs(fc, K0 + vc * 128, 128)
                    e.ldweights(lt)
                    mi = e.matmul(psA[:, 0:128], lt,
                                  wqb[:, fc * 384 + 256: fc * 384 + 384],
                                  start=(fc == 0), stop=(fc == FC - 1))
                return mi

            @p.step("scalar")
            def _(e, H, vc=vc):
                e.activation(out=vtA[:, vc * 65: vc * 65 + 64], in_=psA[:, 0:64],
                             func=AF.Identity)
                return e.activation(out=vtB[:, vc * 65: vc * 65 + 64], in_=psA[:, 64:128],
                                    func=AF.Identity)

        emit_head(0)
        emit_head(1)

        for c0, n in PKBL:
            @p.step("tensor")
            def _(e, H, c0=c0, n=n):
                e.ldweights(selm[:, 0:128])
                e.matmul(psA[:, 0:n], selm[:, 0:128], avh[:, c0:c0 + n],
                         start=True, stop=False)
                e.ldweights(selm[:, 256:384])
                return e.matmul(psA[:, 0:n], selm[:, 256:384], avh[:, TQ + c0:TQ + c0 + n],
                                start=False, stop=True)

            @p.step("scalar")
            def _(e, H, n=n):
                return e.activation(out=avst[:, 0:n], in_=psA[:, 0:n], func=AF.Identity)

            @p.step("sync", ndma=1)
            def _(e, H, c0=c0, n=n, hp=hp):
                u0 = c0 - (QEXT - 1)
                return [e.dma_start(out=avd[:, hp:hp + 1, u0:u0 + n],
                                    in_=avst[:, 0:n].rearrange("q (c t) -> q c t", c=1))]

    for hp in range(NPAIR):
        emit_pair(hp)

    for tl in (qpair, kpair, rotu, rot2, cms, sms, qB, wqb, vtA, vtB, pTt,
               av1, lbt, avh, msB, mAs, avst, zb):
        tl.free()
    nrm.free(); mtv.free(); lrec.free()
    avs = arB.alloc(IC * 512)
    wob = [arB.alloc(IC * 128), arB.alloc(IC * 128)]

    WOBL = [(0, 512), (512, 512), (1024, 512), (1536, 256), (1792, 257)]
    for u0, n in WOBL:
        @p.step("sync", ndma=1)
        def _(e, H, u0=u0, n=n):
            return [e.dma_start(out=avs.ap().rearrange("q (c t) -> q c t", c=IC)[:, :, 0:n],
                                in_=avd[:, :, u0:u0 + n])]

        for oc in range(FC):
            @p.step("gpsimd", ndma=1)
            def _(e, H, oc=oc):
                return [e.dma_start(
                    out=wob[oc % 2].ap().rearrange("q (c d) -> q c d", c=IC),
                    in_=wo[:, :, oc * 128:(oc + 1) * 128])]

            @p.step("tensor")
            def _(e, H, oc=oc, n=n):
                bk = (oc % 2) * 512
                for ic in range(IC):
                    lt = wob[oc % 2][:, ic * 128:(ic + 1) * 128]
                    e.ldweights(lt)
                    mi = e.matmul(psA[:, bk:bk + n], lt, avs[:, ic * 512:ic * 512 + n],
                                  start=(ic == 0), stop=(ic == IC - 1))
                return mi

            @p.step("vector")
            def _(e, H, oc=oc, u0=u0, n=n):
                bk = (oc % 2) * 512
                if u0 == 0:
                    vgate(e, H, e.tensor_tensor(out=y1h[:, oc:oc + 1],
                                                in0=y1h[:, oc:oc + 1],
                                                in1=psA[:, bk:bk + 1], op=ALU.add))
                    sl = y[:, oc * T: oc * T + n - 1]
                    return e.tensor_tensor(out=sl, in0=sl, in1=psA[:, bk + 1:bk + n],
                                           op=ALU.add)
                sl = y[:, oc * T + u0 - 1: oc * T + u0 - 1 + n]
                return e.tensor_tensor(out=sl, in0=sl, in1=psA[:, bk:bk + n], op=ALU.add)

    avs.free(); wob[0].free(); wob[1].free()
    z2b = arB.alloc(FC * 512)
    ht = arB.alloc(HIDC * 512)
    w1b = [arB.alloc(FC * 256), arB.alloc(FC * 256)]
    w2b = [arB.alloc(HIDC * 128), arB.alloc(HIDC * 128)]
    gta = arB.alloc(512)
    gtb = arB.alloc(512)
    zshb = arF.alloc(FC * 512)
    rws = arF.alloc(512)

    for b0 in range(0, T, 512):
        @p.step("scalar")
        def _(e, H, b0=b0):
            for fc in range(FC):
                if fc < 4:
                    mi = e.activation(out=zshb[:, fc * 512:(fc + 1) * 512],
                                      in_=y[:, fc * T + b0: fc * T + b0 + 512],
                                      func=AF.Identity)
                else:
                    mi = e.activation(out=zshb[:, fc * 512 + 1:(fc + 1) * 512],
                                      in_=y[:, fc * T + b0: fc * T + b0 + 511],
                                      func=AF.Identity)
            return mi

        @p.step("vector")
        def _(e, H, b0=b0):
            for fc in range(4, FC):
                vgate(e, H, e.tensor_copy(zshb[:, fc * 512:fc * 512 + 1],
                                          y1h[:, fc:fc + 1]))
            for fc in range(4, FC):
                vgate(e, H, e.tensor_copy(y1h[:, fc:fc + 1],
                                          y[:, fc * T + b0 + 511: fc * T + b0 + 512]))
            return e.tensor_copy(zt[:, 0:256], zt[:, 0:256])

        for fc in range(FC):
            @p.step("scalar")
            def _(e, H, fc=fc):
                return e.activation(out=sqs[:, 0:512],
                                    in_=zshb[:, fc * 512:(fc + 1) * 512], func=AF.Square)

            @p.step("tensor")
            def _(e, H, fc=fc):
                e.matmul(psA[0:1, 2 * 512:2 * 512 + 512], onesp[:, 0:1],
                         zshb[:, fc * 512:(fc + 1) * 512],
                         start=(fc == 0), stop=(fc == FC - 1))
                return e.matmul(psA[0:1, 3 * 512:3 * 512 + 512], onesp[:, 0:1],
                                sqs[:, 0:512], start=(fc == 0), stop=(fc == FC - 1))

        @p.step("vector")
        def _(e, H):
            vgate(e, H, e.tensor_scalar_mul(rws[0:1, 0:512],
                                            psA[0:1, 2 * 512:2 * 512 + 512],
                                            ccs[0:1, C2_INVD:C2_INVD + 1]))
            vgate(e, H, e.tensor_scalar_mul(zt[0:1, 0:512],
                                            psA[0:1, 3 * 512:3 * 512 + 512],
                                            ccs[0:1, C2_INVD:C2_INVD + 1]))
            vgate(e, H, e.tensor_tensor(out=sqs[0:1, 0:512], in0=rws[0:1, 0:512],
                                        in1=rws[0:1, 0:512], op=ALU.mult))
            vgate(e, H, e.tensor_tensor(out=sqs[0:1, 0:512], in0=zt[0:1, 0:512],
                                        in1=sqs[0:1, 0:512], op=ALU.subtract))
            return e.tensor_copy(zt[:, 0:256], zt[:, 0:256])

        @p.step("scalar")
        def _(e, H):
            return e.activation(out=sqs[0:1, 0:512], in_=sqs[0:1, 0:512], func=AF.Sqrt,
                                bias=ccs[0:1, C2_LNEPS:C2_LNEPS + 1])

        @p.step("vector")
        def _(e, H):
            vgate(e, H, e.reciprocal(sqs[0:1, 0:512], sqs[0:1, 0:512]))
            return e.tensor_copy(zt[:, 0:256], zt[:, 0:256])

        @p.step("tensor")
        def _(e, H):
            e.matmul(psA[:, 2 * 512:2 * 512 + 512], spread[0:1, 256:384],
                     rws[0:1, 0:512], start=True, stop=True)
            return e.matmul(psA[:, 3 * 512:3 * 512 + 512], spread[0:1, 256:384],
                            sqs[0:1, 0:512], start=True, stop=True)

        for fc in range(FC):
            @p.step("vector")
            def _(e, H, fc=fc):
                sl = zshb[:, fc * 512:(fc + 1) * 512]
                e.tensor_tensor(out=sl, in0=sl, in1=psA[:, 2 * 512:2 * 512 + 512],
                                op=ALU.subtract)
                return e.tensor_tensor(out=sl, in0=sl, in1=psA[:, 3 * 512:3 * 512 + 512],
                                       op=ALU.mult)

            @p.step("scalar")
            def _(e, H, fc=fc):
                return e.activation(out=z2b[:, fc * 512:(fc + 1) * 512],
                                    in_=zshb[:, fc * 512:(fc + 1) * 512],
                                    func=AF.Identity,
                                    scale=ccs[:, C2_LG + fc:C2_LG + fc + 1],
                                    bias=ccs[:, C2_LB + fc:C2_LB + fc + 1])

        for j in range(HIDC):
            @p.step("gpsimd", ndma=2)
            def _(e, H, j=j):
                wv = w1b[j % 2].ap().rearrange("q (c d) -> q c d", c=FC)
                return [e.dma_start(out=wv[:, :, 0:128],
                                    in_=w1[:, :, j * 128:(j + 1) * 128]),
                        e.dma_start(out=wv[:, :, 128:256],
                                    in_=w1[:, :, HID2 + j * 128: HID2 + (j + 1) * 128])]

            @p.step("tensor")
            def _(e, H, j=j):
                for fc in range(FC):
                    lt = w1b[j % 2][:, fc * 256: fc * 256 + 128]
                    e.ldweights(lt)
                    e.matmul(psA[:, 0:512], lt, z2b[:, fc * 512:(fc + 1) * 512],
                             start=(fc == 0), stop=(fc == FC - 1))
                for fc in range(FC):
                    lt = w1b[j % 2][:, fc * 256 + 128: fc * 256 + 256]
                    e.ldweights(lt)
                    mi = e.matmul(psA[:, 512:1024], lt, z2b[:, fc * 512:(fc + 1) * 512],
                                  start=(fc == 0), stop=(fc == FC - 1))
                return mi

            @p.step("scalar")
            def _(e, H, j=j):
                e.activation(out=gta[:, 0:512], in_=psA[:, 512:1024], func=AF.Gelu,
                             bias=b1s[:, HIDC + j: HIDC + j + 1])
                return e.activation(out=gtb[:, 0:512], in_=psA[:, 0:512],
                                    func=AF.Identity, bias=b1s[:, j:j + 1])

            @p.step("vector")
            def _(e, H, j=j):
                return e.tensor_tensor(out=ht[:, j * 512:(j + 1) * 512],
                                       in0=gta[:, 0:512], in1=gtb[:, 0:512], op=ALU.mult)

        for oc in range(FC):
            @p.step("gpsimd", ndma=1)
            def _(e, H, oc=oc):
                return [e.dma_start(
                    out=w2b[oc % 2].ap().rearrange("q (c d) -> q c d", c=HIDC),
                    in_=w2[:, :, oc * 128:(oc + 1) * 128])]

            @p.step("tensor")
            def _(e, H, oc=oc):
                for j in range(HIDC):
                    lt = w2b[oc % 2][:, j * 128:(j + 1) * 128]
                    e.ldweights(lt)
                    mi = e.matmul(psA[:, 2 * 512:2 * 512 + 512], lt,
                                  ht[:, j * 512:(j + 1) * 512],
                                  start=(j == 0), stop=(j == HIDC - 1))
                return mi

            @p.step("vector")
            def _(e, H, oc=oc, b0=b0):
                sl = y[:, oc * T + b0: oc * T + b0 + 512]
                return e.scalar_tensor_tensor(out=sl, in0=psA[:, 2 * 512:2 * 512 + 512],
                                              scalar=ccs[:, C2_B2 + oc:C2_B2 + oc + 1],
                                              op0=ALU.add, in1=sl, op1=ALU.add)

    @p.step("sync", ndma=1)
    def _(e, H):
        return [e.dma_start(out=yout, in_=y.ap().rearrange("q (c t) -> q c t", c=FC))]

    @p.step("vector")
    def _(e, H):
        return e.memset(zt[:, 0:256], 0.0)

    p.emit(ctx)
    return nc, ctx


def host_prep_ffn_weights(MOD, P):
    M = ffn_mod_dims(MOD)
    INNER = M["INNER"]; NPAIR = M["NPAIR"]; HIDC = M["HIDC"]
    HID2 = M["HID2"]; HID = M["HID"]
    Wq = np.asarray(P["wqkv"], np.float32)
    slabs = []
    for hp in range(NPAIR):
        c = 2 * hp * 64
        slab = np.concatenate([Wq[:, c:c + 128], Wq[:, INNER + c:INNER + c + 128],
                               Wq[:, 2 * INNER + c:2 * INNER + c + 128]], axis=1)
        slabs.append(kmaj(slab))
    o = {"wqkv": np.ascontiguousarray(np.stack(slabs))}
    o["wo"] = kmaj(np.asarray(P["wo"], np.float32))
    w1p = np.zeros((D, 2 * HID2), np.float32)
    w1p[:, 0:HID] = P["w1"][:, 0:HID]
    w1p[:, HID2:HID2 + HID] = P["w1"][:, HID:2 * HID]
    o["w1"] = kmaj(w1p)
    w2p = np.zeros((HID2, D), np.float32)
    w2p[0:HID] = P["w2"]
    o["w2"] = kmaj(w2p)
    b1c = np.zeros((128, 2 * HIDC), np.float32)
    bpad = np.zeros(HID2, np.float32); bpad[0:HID] = P["b1"][0:HID]
    b1c[:, 0:HIDC] = bpad.reshape(HIDC, 128).T
    bpad = np.zeros(HID2, np.float32); bpad[0:HID] = P["b1"][HID:2 * HID]
    b1c[:, HIDC:2 * HIDC] = bpad.reshape(HIDC, 128).T
    o["b1c"] = b1c
    cc2 = np.zeros((128, 64), np.float32)
    cc2[:, 0:8] = colv(P["ln_g"])
    cc2[:, 8:16] = colv(P["ln_b"])
    cc2[:, 16:24] = colv(P["b2"])
    cc2[:, 24] = 1e-12
    cc2[:, 25] = -1.0
    cc2[:, 26] = 8.0
    cc2[:, 27] = 1.0 / D
    cc2[:, 28] = 1e-30
    cc2[:, 29] = 1e-5
    o["cc2"] = cc2
    hsp = np.zeros((128, 2), np.float32)
    hsp[0:64, 0] = 1.0; hsp[64:128, 1] = 1.0
    o["hsplit"] = hsp
    qs = np.asarray(P["q_scale"], np.float32).reshape(-1)
    ks = np.asarray(P["k_scale"], np.float32).reshape(-1)
    spr = np.zeros((128, 512), np.float32)
    pp = np.arange(128)
    for r in range(2):
        mask = (pp // 64) == r
        spr[r, 0:128][mask] = qs[pp[mask] % 64]
        spr[r, 128:256][mask] = ks[pp[mask] % 64]
    spr[0, 256:384] = 1.0
    spr[1, 384:512] = 1.0
    o["spread"] = spr
    sel = np.zeros((128, 384), np.float32)
    for i in range(64):
        sel[i, i] = 1.0
        sel[i + 64, 128 + i] = 1.0
        sel[i, 256 + 64 + i] = 1.0
    o["sel"] = sel.astype(bfloat16)
    o["ident"] = np.eye(128, dtype=np.float32)
    return o


def host_prep_ffn_parity(MOD, h):
    M = ffn_mod_dims(MOD)
    W = M["W"]; SPAN = M["SPAN"]; NKC = M["NKC"]; NSPEC = M["NSPEC"]
    KH = M["KH"]; TQ = M["TQ"]; TKY = M["TKY"]; QEXT = M["QEXT"]; QB = M["QB"]
    NEG = np.float32(-1e30)
    pp = np.arange(128)[:, None]
    qc = np.arange(QB)[None, :]
    mB = np.zeros((128, (NSPEC + 1) * NKC * QB), np.float32)
    for s in range(NSPEC + 1):
        for kc in range(NKC):
            dlt = kc * 128 + pp - qc
            valid = (dlt >= 0) & (dlt <= W)
            if s < NSPEC and h == 0:
                valid = valid & (s * QB + kc * 128 + pp - KH >= 0)
            mB[:, (s * NKC + kc) * QB:(s * NKC + kc + 1) * QB] = np.where(valid, 0.0, NEG)
    col = np.arange(SPAN)[None, :]
    mA = np.zeros((128, (NSPEC + 1) * 2 * SPAN), np.float32)
    for s in range(NSPEC + 1):
        for sub in range(2):
            qrel = sub * 128 + pp
            valid = (col >= qrel) & (col <= qrel + W)
            if s < NSPEC and h == 0:
                valid = valid & (s * QB + col - KH >= 0)
            mA[:, (s * 2 + sub) * SPAN:(s * 2 + sub + 1) * SPAN] = np.where(valid, 0.0, NEG)
    j32 = np.arange(128) % 32
    invf = 1.0 / 10000.0 ** (2 * j32 / 64.0)
    base = (2 * j32 + 0.4 * 64) / (1.4 * 64)
    sign = np.where(np.arange(128) % 64 < 32, -1.0, 1.0)

    def maps(tok, inv):
        ang = tok[None, :] * invf[:, None]
        power = (tok - TF // 2) / (W // 2.0)
        scale = base[:, None] ** power[None, :]
        if inv:
            scale = 1.0 / scale
        return ((np.cos(ang) * scale).astype(np.float32),
                (sign[:, None] * np.sin(ang) * scale).astype(np.float32))

    tq = (h * T + np.arange(TQ) - QEXT).astype(np.float64)
    tk = (h * T + np.arange(TKY) - KH).astype(np.float64)
    rqc, rqs = maps(tq, False)
    rkc, rks = maps(tk, True)
    return {"mstr": mB.astype(bfloat16), "mstrA": mA.astype(bfloat16),
            "rqc": rqc, "rqs": rqs, "rkc": rkc, "rks": rks}


def host_prep_ffn_core(MOD, core, cur, mustab):
    M = ffn_mod_dims(MOD)
    HB = M["HB"]
    b, h = core // 2, core % 2
    o = {"yin": to_fm(cur[b, h * T:(h + 1) * T])}
    if h == 0:
        o["hal"] = np.zeros((128, FC, HB), np.float32)
        halmu = np.zeros((2, HB), np.float32)
        halmu[1] = 1.0
    else:
        o["hal"] = to_fm(cur[b, T - HB:T])
        if mustab is None:
            halmu = np.zeros((2, HB), np.float32); halmu[1] = 1.0
        else:
            halmu = np.asarray(mustab[b][:, T - HB:T], np.float32)
    if mustab is None:
        ownmu = np.zeros((2, T), np.float32); ownmu[1] = 1.0
    else:
        ownmu = np.asarray(mustab[b][:, h * T:(h + 1) * T], np.float32)
    o["mus"] = np.ascontiguousarray(np.concatenate([halmu, ownmu], axis=1))
    return o


def build_phase_C():
    TH, IC3 = 16, 8

    nc = bass.Bass("TRN2", target_bir_lowering=False, detect_race_conditions=False)

    def dt_in(name, shape, dt=F32):
        return nc.dram_tensor(name, shape, dt, kind="ExternalInput").ap()
    xf = dt_in("xf", [128, FC, TF])
    wq3 = dt_in("wq3", [128, 128]); wk3 = dt_in("wk3", [128, 128]); wv3 = dt_in("wv3", [128, 128])
    wo3 = dt_in("wo3", [128, IC3, D])
    cc3 = dt_in("cc3", [128, 32])
    tri1_in = dt_in("tri1", [128, 128], BF16)
    trm_in = dt_in("trm", [128, 4 * 512], BF16)
    ident_in = dt_in("ident3", [128, 128])
    s1out = nc.dram_tensor("s1out", [128, FC, T], F32, kind="ExternalOutput").ap()
    st1out = nc.dram_tensor("st1out", [2, TF], F32, kind="ExternalOutput").ap()

    C3_NEG1, C3_TINY, C3_SC, C3_INVD, C3_LNEPS, C3_POM, C3_K10 = 0, 1, 2, 3, 4, 5, 6

    p = SProg(nc)
    ctx = ExitStack()
    ccs = ctx.enter_context(nc.sbuf_tensor("ccs3", [128, 32], F32))
    ident = ctx.enter_context(nc.sbuf_tensor("ident3s", [128, 128], F32))
    zt = ctx.enter_context(nc.sbuf_tensor("zt3", [128, 512], F32))
    wqs = ctx.enter_context(nc.sbuf_tensor("wqs", [128, 128], BF16))
    wks = ctx.enter_context(nc.sbuf_tensor("wks", [128, 128], BF16))
    wvs = ctx.enter_context(nc.sbuf_tensor("wvs", [128, 128], BF16))
    arF_t = ctx.enter_context(nc.sbuf_tensor("arF3", [128, 6500], F32))
    arB_t = ctx.enter_context(nc.sbuf_tensor("arB3", [128, 70400], BF16))
    psA = ctx.enter_context(nc.psum_tensor("psA3", [128, 8 * 512], F32))
    arF = Arena(arF_t, 6500)
    arB = Arena(arB_t, 70400)

    xb = arF.alloc(FC * 512)
    rws = arF.alloc(512)
    sq3 = arF.alloc(512)
    mtv = arF.alloc(16)
    lrec = arF.alloc(512)
    s1s = arF.alloc(512)
    onespC = arF.alloc(128)

    xnt = arB.alloc(FC * TF)
    qdm = arB.alloc(T)
    kdm = arB.alloc(TF)
    vtm = arB.alloc(32 * 65)
    pT = arB.alloc(1024)
    avC = arB.alloc(IC3 * T)
    avb = arB.alloc(512)
    lbb = arB.alloc(512)
    avn = arB.alloc(512)
    trm = arB.alloc(4 * 512)
    tri1 = arB.alloc(128)
    wo3b = arB.alloc(IC3 * D)

    @p.step("sync", ndma=4)
    def _(e, H):
        return [e.dma_start(out=ccs[:], in_=cc3),
                e.dma_start(out=ident[:], in_=ident_in),
                e.dma_start(out=tri1.ap(), in_=tri1_in),
                e.dma_start(out=trm.ap(), in_=trm_in)]

    @p.step("gpsimd", ndma=4)
    def _(e, H):
        return [e.dma_start(out=wqs[:], in_=wq3),
                e.dma_start(out=wks[:], in_=wk3),
                e.dma_start(out=wvs[:], in_=wv3),
                e.dma_start(out=wo3b.ap().rearrange("q (c d) -> q c d", c=IC3), in_=wo3)]

    @p.step("vector")
    def _(e, H):
        return e.memset(onespC.ap(), 1.0)

    for b0 in range(0, TF, 512):
        @p.step("sync", ndma=1)
        def _(e, H, b0=b0):
            return [e.dma_start(out=xb.ap().rearrange("q (c t) -> q c t", c=FC),
                                in_=xf[:, :, b0:b0 + 512])]

        for fc in range(FC):
            @p.step("scalar")
            def _(e, H, fc=fc):
                return e.activation(out=sq3[:, 0:512], in_=xb[:, fc * 512:(fc + 1) * 512],
                                    func=AF.Square)

            @p.step("tensor")
            def _(e, H, fc=fc):
                e.matmul(psA[0:1, 2 * 512:2 * 512 + 512], onespC[:, 0:1],
                         xb[:, fc * 512:(fc + 1) * 512],
                         start=(fc == 0), stop=(fc == FC - 1))
                return e.matmul(psA[0:1, 3 * 512:3 * 512 + 512], onespC[:, 0:1],
                                sq3[:, 0:512], start=(fc == 0), stop=(fc == FC - 1))

        @p.step("vector")
        def _(e, H):
            vgate(e, H, e.tensor_scalar_mul(rws[0:1, 0:512],
                                            psA[0:1, 2 * 512:2 * 512 + 512],
                                            ccs[0:1, C3_INVD:C3_INVD + 1]))
            vgate(e, H, e.tensor_scalar_mul(zt[0:1, 0:512],
                                            psA[0:1, 3 * 512:3 * 512 + 512],
                                            ccs[0:1, C3_INVD:C3_INVD + 1]))
            vgate(e, H, e.tensor_tensor(out=sq3[0:1, 0:512], in0=rws[0:1, 0:512],
                                        in1=rws[0:1, 0:512], op=ALU.mult))
            vgate(e, H, e.tensor_tensor(out=sq3[0:1, 0:512], in0=zt[0:1, 0:512],
                                        in1=sq3[0:1, 0:512], op=ALU.subtract))
            return e.tensor_copy(zt[:, 0:256], zt[:, 0:256])

        @p.step("scalar")
        def _(e, H):
            return e.activation(out=sq3[0:1, 0:512], in_=sq3[0:1, 0:512], func=AF.Sqrt,
                                bias=ccs[0:1, C3_LNEPS:C3_LNEPS + 1])

        @p.step("vector")
        def _(e, H):
            vgate(e, H, e.reciprocal(sq3[0:1, 0:512], sq3[0:1, 0:512]))
            return e.tensor_copy(zt[:, 0:256], zt[:, 0:256])

        @p.step("sync", ndma=2)
        def _(e, H, b0=b0):
            return [e.dma_start(out=st1out[0:1, b0:b0 + 512], in_=rws[0:1, 0:512]),
                    e.dma_start(out=st1out[1:2, b0:b0 + 512], in_=sq3[0:1, 0:512])]

        @p.step("tensor")
        def _(e, H):
            e.matmul(psA[:, 2 * 512:2 * 512 + 512], onespC[0:1, 0:128],
                     rws[0:1, 0:512], start=True, stop=True)
            return e.matmul(psA[:, 3 * 512:3 * 512 + 512], onespC[0:1, 0:128],
                            sq3[0:1, 0:512], start=True, stop=True)

        for fc in range(FC):
            @p.step("vector")
            def _(e, H, fc=fc):
                sl = xb[:, fc * 512:(fc + 1) * 512]
                e.tensor_tensor(out=sl, in0=sl, in1=psA[:, 2 * 512:2 * 512 + 512],
                                op=ALU.subtract)
                return e.tensor_tensor(out=sl, in0=sl, in1=psA[:, 3 * 512:3 * 512 + 512],
                                       op=ALU.mult)

            @p.step("scalar")
            def _(e, H, fc=fc, b0=b0):
                return e.activation(out=xnt[:, fc * TF + b0: fc * TF + b0 + 512],
                                    in_=xb[:, fc * 512:(fc + 1) * 512], func=AF.Identity)

    def emit_head3(h):
        hp, b64 = h // 2, 64 * (h % 2)

        @p.step("vector")
        def _(e, H):
            vgate(e, H, e.memset(qdm[64:66, 0:T], 1.0))
            vgate(e, H, e.tensor_scalar_mul(qdm[64:66, 0:T], qdm[64:66, 0:T],
                                            ccs[64:66, C3_POM:C3_POM + 1]))
            vgate(e, H, e.memset(kdm[64:66, 0:TF], 1.0))
            vgate(e, H, e.tensor_scalar_mul(kdm[64:66, 2048:TF], kdm[64:66, 2048:TF],
                                            ccs[64:66, C3_K10:C3_K10 + 1]))
            return e.memset(vtm.ap(), 1.0)

        for tc in range(0, T, 512):
            @p.step("tensor")
            def _(e, H, tc=tc):
                lt = wqs[:, b64:b64 + 64]
                e.ldweights(lt)
                return e.matmul(psA[0:64, 512:1024], lt,
                                xnt[:, hp * TF + tc: hp * TF + tc + 512],
                                start=True, stop=True)

            @p.step("scalar")
            def _(e, H, tc=tc):
                return e.activation(out=qdm[0:64, tc:tc + 512], in_=psA[0:64, 512:1024],
                                    func=AF.Identity)

        for kb in range(0, TF, 512):
            src = T + kb if kb < 2048 else kb - 2048
            @p.step("tensor")
            def _(e, H, src=src):
                lt = wks[:, b64:b64 + 64]
                e.ldweights(lt)
                return e.matmul(psA[0:64, 512:1024], lt,
                                xnt[:, hp * TF + src: hp * TF + src + 512],
                                start=True, stop=True)

            @p.step("scalar")
            def _(e, H, kb=kb):
                return e.activation(out=kdm[0:64, kb:kb + 512], in_=psA[0:64, 512:1024],
                                    func=AF.Identity)

        for g in range(4):
            @p.step("tensor")
            def _(e, H, g=g):
                for i in range(8):
                    vc = g * 8 + i
                    src = T + vc * 128 if vc < 16 else (vc - 16) * 128
                    lt = xnt[:, hp * TF + src: hp * TF + src + 128]
                    e.ldweights(lt)
                    mi = e.matmul(psA[:, i * 64:(i + 1) * 64], lt, wvs[:, b64:b64 + 64],
                                  start=(i == 0), stop=True)
                return mi

            @p.step("scalar")
            def _(e, H, g=g):
                for i in range(8):
                    vc = g * 8 + i
                    mi = e.activation(out=vtm[:, vc * 65: vc * 65 + 64],
                                      in_=psA[:, i * 64:(i + 1) * 64], func=AF.Identity)
                return mi

        for qt in range(16):
            kend = 2176 + qt * 128

            @p.step("tensor")
            def _(e, H, qt=qt, kend=kend):
                e.ldweights(qdm[0:66, qt * 128:(qt + 1) * 128])
                done = 0
                while done < kend:
                    n = min(512, kend - done)
                    mi = e.matmul(psA[:, done:done + n], qdm[0:66, qt * 128:(qt + 1) * 128],
                                  kdm[0:66, done:done + n], start=True, stop=True)
                    done += n
                return mi

            @p.step("vector")
            def _(e, H, kend=kend):
                e.tensor_tensor(out=psA[:, kend - 128:kend], in0=psA[:, kend - 128:kend],
                                in1=tri1.ap(), op=ALU.add)
                return e.tensor_reduce(out=mtv[:, 0:1], in_=psA[:, 0:kend],
                                       op=ALU.max, axis=AX.X)

            @p.step("tensor")
            def _(e, H):
                return e.matmul(psA[64:65, 0:128], mtv[:, 0:1], ident[:, 0:128],
                                start=True, stop=True)

            @p.step("scalar")
            def _(e, H, qt=qt):
                return e.activation(out=qdm[64:65, qt * 128:(qt + 1) * 128],
                                    in_=psA[64:65, 0:128], func=AF.Identity,
                                    scale=ccs[64:65, C3_NEG1:C3_NEG1 + 1])

        for qb2 in range(4):
            KT = 16 + (qb2 + 1) * 4
            DIAG0 = 16 + qb2 * 4
            kt = 0
            while kt < KT:
                if kt + 1 < DIAG0:
                    @p.step("tensor")
                    def _(e, H, kt=kt, qb2=qb2):
                        e.ldweights(kdm[0:66, kt * 128:(kt + 1) * 128])
                        e.matmul(psA[:, 3 * 512:4 * 512], kdm[0:66, kt * 128:(kt + 1) * 128],
                                 qdm[0:66, qb2 * 512:(qb2 + 1) * 512], start=True, stop=True)
                        e.ldweights(kdm[0:66, (kt + 1) * 128:(kt + 2) * 128])
                        return e.matmul(psA[:, 4 * 512:5 * 512],
                                        kdm[0:66, (kt + 1) * 128:(kt + 2) * 128],
                                        qdm[0:66, qb2 * 512:(qb2 + 1) * 512],
                                        start=True, stop=True)

                    @p.step("scalar")
                    def _(e, H):
                        return e.activation(out=pT[:, 0:1024], in_=psA[:, 3 * 512:5 * 512],
                                            func=AF.Exp, scale=ccs[:, C3_SC:C3_SC + 1])

                    @p.step("tensor")
                    def _(e, H, kt=kt, KT=KT):
                        e.ldweights(vtm[:, kt * 65:(kt + 1) * 65])
                        e.matmul(psA[0:65, 5 * 512:6 * 512], vtm[:, kt * 65:(kt + 1) * 65],
                                 pT[:, 0:512], start=(kt == 0), stop=False)
                        e.ldweights(vtm[:, (kt + 1) * 65:(kt + 2) * 65])
                        return e.matmul(psA[0:65, 5 * 512:6 * 512],
                                        vtm[:, (kt + 1) * 65:(kt + 2) * 65],
                                        pT[:, 512:1024], start=False,
                                        stop=(kt + 2 == KT))
                    kt += 2
                else:
                    diag = kt >= DIAG0

                    @p.step("tensor")
                    def _(e, H, kt=kt, qb2=qb2):
                        e.ldweights(kdm[0:66, kt * 128:(kt + 1) * 128])
                        return e.matmul(psA[:, 3 * 512:4 * 512],
                                        kdm[0:66, kt * 128:(kt + 1) * 128],
                                        qdm[0:66, qb2 * 512:(qb2 + 1) * 512],
                                        start=True, stop=True)

                    if diag:
                        @p.step("vector")
                        def _(e, H, kt=kt, DIAG0=DIAG0):
                            rel = kt - DIAG0
                            return e.tensor_tensor(out=psA[:, 3 * 512:4 * 512],
                                                   in0=psA[:, 3 * 512:4 * 512],
                                                   in1=trm[:, rel * 512:(rel + 1) * 512],
                                                   op=ALU.add)

                    @p.step("scalar")
                    def _(e, H):
                        return e.activation(out=pT[:, 0:512], in_=psA[:, 3 * 512:4 * 512],
                                            func=AF.Exp, scale=ccs[:, C3_SC:C3_SC + 1])

                    @p.step("tensor")
                    def _(e, H, kt=kt, KT=KT):
                        e.ldweights(vtm[:, kt * 65:(kt + 1) * 65])
                        return e.matmul(psA[0:65, 5 * 512:6 * 512],
                                        vtm[:, kt * 65:(kt + 1) * 65], pT[:, 0:512],
                                        start=(kt == 0), stop=(kt + 1 == KT))
                    kt += 1

            @p.step("vector")
            def _(e, H):
                vgate(e, H, e.tensor_scalar_add(lrec[64:65, 0:512],
                                                psA[64:65, 5 * 512:6 * 512],
                                                ccs[64:65, C3_TINY:C3_TINY + 1]))
                vgate(e, H, e.reciprocal(lrec[64:65, 0:512], lrec[64:65, 0:512]))
                return e.tensor_copy(zt[:, 0:256], zt[:, 0:256])

            @p.step("tensor")
            def _(e, H):
                return e.matmul(psA[0:64, 6 * 512:7 * 512], onespC[64:65, 0:64],
                                lrec[64:65, 0:512], start=True, stop=True)

            @p.step("scalar")
            def _(e, H):
                e.activation(out=avb[0:64, 0:512], in_=psA[0:64, 5 * 512:6 * 512],
                             func=AF.Identity)
                return e.activation(out=lbb[0:64, 0:512], in_=psA[0:64, 6 * 512:7 * 512],
                                    func=AF.Identity)

            @p.step("vector")
            def _(e, H):
                return e.tensor_tensor(out=avn[0:64, 0:512], in0=avb[0:64, 0:512],
                                       in1=lbb[0:64, 0:512], op=ALU.mult)

            if b64 == 0:
                @p.step("vector")
                def _(e, H, qb2=qb2):
                    return e.tensor_copy(avC[0:64, hp * T + qb2 * 512: hp * T + qb2 * 512 + 512],
                                         avn[0:64, 0:512])
            else:
                @p.step("gpsimd", ndma=1)
                def _(e, H, qb2=qb2):
                    return [e.dma_start(
                        out=avC[64:128, hp * T + qb2 * 512: hp * T + qb2 * 512 + 512],
                        in_=avn[0:64, 0:512])]

    for h in range(TH):
        emit_head3(h)

    for qb2 in range(4):
        for oc in range(FC):
            @p.step("tensor")
            def _(e, H, qb2=qb2, oc=oc):
                bk = 2 * 512 + (oc % 2) * 512
                for ic in range(IC3):
                    lt = wo3b[:, ic * D + oc * 128: ic * D + oc * 128 + 128]
                    e.ldweights(lt)
                    mi = e.matmul(psA[:, bk:bk + 512], lt,
                                  avC[:, ic * T + qb2 * 512: ic * T + qb2 * 512 + 512],
                                  start=(ic == 0), stop=(ic == IC3 - 1))
                return mi

            @p.step("scalar")
            def _(e, H, oc=oc):
                bk = 2 * 512 + (oc % 2) * 512
                return e.activation(out=s1s[:, 0:512], in_=psA[:, bk:bk + 512],
                                    func=AF.Identity)

            @p.step("sync", ndma=1)
            def _(e, H, qb2=qb2, oc=oc):
                return [e.dma_start(
                    out=s1out[:, oc:oc + 1, qb2 * 512:(qb2 + 1) * 512],
                    in_=s1s[:, 0:512].rearrange("q (c t) -> q c t", c=1))]

    @p.step("vector")
    def _(e, H):
        return e.memset(zt[:, 0:256], 0.0)

    p.emit(ctx)
    return nc, ctx


def host_prep_C(P, core):
    b, h = core // 2, core % 2
    o = {}
    def bdiag(Wm):
        t = np.asarray(Wm, np.float32).T
        out = np.zeros((128, 128), np.float32)
        out[0:64, 0:64] = t
        out[64:128, 64:128] = t
        return out
    o["wq3"] = bdiag(P["Wq"])
    o["wk3"] = bdiag(P["Wk"])
    o["wv3"] = bdiag(P["Wv"])
    o["wo3"] = kmaj(np.ascontiguousarray(np.asarray(P["Wo"], np.float32).T))
    cc3 = np.zeros((128, 32), np.float32)
    cc3[:, 0] = -1.0
    cc3[:, 1] = 1e-30
    cc3[:, 2] = 0.125
    cc3[:, 3] = 1.0 / D
    cc3[:, 4] = 1e-5
    cc3[:, 5] = 0.0 if h == 1 else -1e30
    cc3[64, 5] = 0.0
    cc3[:, 6] = 0.0
    cc3[64, 6] = 1.0
    o["cc3"] = cc3
    NEG = np.float32(-1e30)
    pp = np.arange(128)[:, None]
    cq = np.arange(128)[None, :]
    o["tri1"] = np.where(cq <= pp, 0.0, NEG).astype(bfloat16)
    qc = np.arange(512)[None, :]
    trm = np.zeros((128, 4 * 512), np.float32)
    for r in range(4):
        trm[:, r * 512:(r + 1) * 512] = np.where(r * 128 + pp <= qc, 0.0, NEG)
    o["trm"] = trm.astype(bfloat16)
    o["ident3"] = np.eye(128, dtype=np.float32)
    return o


def build_gate():
    nc = bass.Bass("TRN2", target_bir_lowering=False, detect_race_conditions=False)
    xg = nc.dram_tensor("xg", [128, FC, T], F32, kind="ExternalInput").ap()
    sg = nc.dram_tensor("sg", [128, FC, T], F32, kind="ExternalInput").ap()
    gg = nc.dram_tensor("gg", [128, FC, T], F32, kind="ExternalInput").ap()
    xo = nc.dram_tensor("xo", [128, FC, T], F32, kind="ExternalOutput").ap()

    p = SProg(nc)
    ctx = ExitStack()
    xt_ = ctx.enter_context(nc.sbuf_tensor("gxt", [128, T], F32))
    st_ = ctx.enter_context(nc.sbuf_tensor("gst", [128, T], F32))
    gt_ = ctx.enter_context(nc.sbuf_tensor("ggt", [128, T], F32))
    sm_ = ctx.enter_context(nc.sbuf_tensor("gsm", [128, T], F32))

    for fc in range(FC):
        @p.step("sync", ndma=3)
        def _(e, H, fc=fc):
            return [e.dma_start(out=xt_[:], in_=xg[:, fc, :]),
                    e.dma_start(out=st_[:], in_=sg[:, fc, :]),
                    e.dma_start(out=gt_[:], in_=gg[:, fc, :])]

        @p.step("scalar")
        def _(e, H):
            return e.activation(out=sm_[:], in_=gt_[:], func=AF.Sigmoid)

        @p.step("vector")
        def _(e, H):
            e.tensor_tensor(out=sm_[:], in0=sm_[:], in1=gt_[:], op=ALU.mult)
            e.tensor_tensor(out=sm_[:], in0=sm_[:], in1=st_[:], op=ALU.mult)
            return e.tensor_tensor(out=sm_[:], in0=sm_[:], in1=xt_[:], op=ALU.add)

        @p.step("sync", ndma=1)
        def _(e, H, fc=fc):
            return [e.dma_start(out=xo[:, fc, :], in_=sm_[:])]

    @p.step("vector")
    def _(e, H):
        return e.memset(sm_[:, 0:256], 0.0)

    p.emit(ctx)
    return nc, ctx


MOD0 = {"W": 512, "H": 8, "INNER": 512, "HID": 2730}
MOD1 = {"W": 256, "H": 16, "INNER": 1024, "HID": 2730}
CORE_IDS = list(range(NCORES))


def _gather_yout(res):
    out = np.empty((4, TF, D), np.float32)
    for c in range(NCORES):
        b, h = c // 2, c % 2
        out[b, h * T:(h + 1) * T] = from_fm(res.results[c]["yout"])
    return out


def _run_ffn_stack(ncf, MOD, layers, cur, mustab):
    par = [host_prep_ffn_parity(MOD, h) for h in (0, 1)]
    for P in layers:
        wts = host_prep_ffn_weights(MOD, P)
        ims = []
        for c in range(NCORES):
            m = dict(wts)
            m.update(par[c % 2])
            m.update(host_prep_ffn_core(MOD, c, cur, mustab))
            ims.append(m)
        res = run_bass_kernel_spmd(ncf, ims, core_ids=CORE_IDS)
        cur = _gather_yout(res)
        mustab = None
    return cur


_DBG = {}


def kernel(**inputs):
    x = np.asarray(inputs["L0_attn_x"], np.float32)

    ncA, ctxA = build_phase_A()
    resA = run_bass_kernel_spmd(ncA, [host_prep_A(inputs, c) for c in range(NCORES)],
                                core_ids=CORE_IDS)
    ctxA.close()
    s0c = [np.asarray(resA.results[c]["s0"], np.float32) for c in range(NCORES)]
    mustab0 = np.zeros((4, 2, TF), np.float32)
    for c in range(NCORES):
        b, h = c // 2, c % 2
        st = resA.results[c]["st"]
        mustab0[b, 0, h * T:(h + 1) * T] = st[0, 0:T]
        mustab0[b, 1, h * T:(h + 1) * T] = st[0, TF:TF + T]

    nc0, ctx0 = build_phase_ffn(MOD0)
    layers0 = [{"wqkv": inputs["L0_ffn_wqkv"][l], "wo": inputs["L0_ffn_wo"][l],
                "w1": inputs["L0_ffn_w1"][l], "b1": inputs["L0_ffn_b1"][l],
                "w2": inputs["L0_ffn_w2"][l], "b2": inputs["L0_ffn_b2"][l],
                "ln_g": inputs["L0_ffn_ln_g"][l], "ln_b": inputs["L0_ffn_ln_b"][l],
                "q_scale": inputs["L0_ffn_q_scale"][l],
                "k_scale": inputs["L0_ffn_k_scale"][l]} for l in range(2)]
    g0pre = _run_ffn_stack(nc0, MOD0, layers0, x, mustab0)
    ctx0.close()
    _DBG["g0pre"] = g0pre; _DBG["s0c"] = s0c

    ncG, ctxG = build_gate()
    ims = []
    for c in range(NCORES):
        b, h = c // 2, c % 2
        ims.append({"xg": to_fm(x[b, h * T:(h + 1) * T]), "sg": s0c[c],
                    "gg": to_fm(g0pre[b, h * T:(h + 1) * T])})
    resG = run_bass_kernel_spmd(ncG, ims, core_ids=CORE_IDS)
    x1 = np.empty((4, TF, D), np.float32)
    for c in range(NCORES):
        b, h = c // 2, c % 2
        x1[b, h * T:(h + 1) * T] = from_fm(resG.results[c]["xo"])

    _DBG["x1"] = x1

    ncC, ctxC = build_phase_C()
    PC = {"Wq": inputs["L1_attn_Wq"], "Wk": inputs["L1_attn_Wk"],
          "Wv": inputs["L1_attn_Wv"], "Wo": inputs["L1_attn_Wo"]}
    ims = []
    for c in range(NCORES):
        b, h = c // 2, c % 2
        m = host_prep_C(PC, c)
        own = to_fm(x1[b, h * T:(h + 1) * T])
        oth = to_fm(x1[b, (1 - h) * T:(2 - h) * T])
        m["xf"] = np.ascontiguousarray(np.concatenate([own, oth], axis=2))
        ims.append(m)
    resC = run_bass_kernel_spmd(ncC, ims, core_ids=CORE_IDS)
    ctxC.close()
    s1c = [np.asarray(resC.results[c]["s1out"], np.float32) for c in range(NCORES)]
    mustab1 = np.zeros((4, 2, TF), np.float32)
    for c in range(NCORES):
        b, h = c // 2, c % 2
        st = resC.results[c]["st1out"]
        mustab1[b, 0, h * T:(h + 1) * T] = st[0, 0:T]
        mustab1[b, 1, h * T:(h + 1) * T] = st[1, 0:T]

    nc1, ctx1 = build_phase_ffn(MOD1)
    layers1 = [{"wqkv": inputs["L1_ffn_Wqkv"][l], "wo": inputs["L1_ffn_Wo"][l],
                "w1": inputs["L1_ffn_W1"][l], "b1": inputs["L1_ffn_b1"][l],
                "w2": inputs["L1_ffn_W2"][l], "b2": inputs["L1_ffn_b2"][l],
                "ln_g": inputs["L1_ffn_ln_g"][l], "ln_b": inputs["L1_ffn_ln_b"][l],
                "q_scale": inputs["L1_ffn_q_scale"][l],
                "k_scale": inputs["L1_ffn_k_scale"][l]} for l in range(6)]
    g1pre = _run_ffn_stack(nc1, MOD1, layers1, x1, mustab1)
    ctx1.close()
    _DBG["g1pre"] = g1pre; _DBG["s1c"] = s1c

    ims = []
    for c in range(NCORES):
        b, h = c // 2, c % 2
        ims.append({"xg": to_fm(x1[b, h * T:(h + 1) * T]), "sg": s1c[c],
                    "gg": to_fm(g1pre[b, h * T:(h + 1) * T])})
    resG2 = run_bass_kernel_spmd(ncG, ims, core_ids=CORE_IDS)
    ctxG.close()
    x2 = np.empty((4, TF, D), np.float32)
    for c in range(NCORES):
        b, h = c // 2, c % 2
        x2[b, h * T:(h + 1) * T] = from_fm(resG2.results[c]["xo"])
    return x2
